# version 10
# baseline (speedup 1.0000x reference)
import math
import numpy as np
import ml_dtypes
import concourse.bass as bass
import concourse.mybir as mybir
from concourse.bass_utils import run_bass_kernel_spmd

F32 = mybir.dt.float32
BF16 = mybir.dt.bfloat16
I32 = mybir.dt.int32
AF = mybir.ActivationFunctionType
ALU = mybir.AluOpType
AX = mybir.AxisListType

NCORES = 8
N = 4096
E = 1024
H = 16
D = 64
HID = 2048
NEXP = 8
TS = 512
NST = N // TS
CH = 128
NCH = TS // CH
PG = 256
NPG = TS // PG
EPS1 = 1e-5
EPS2 = 1e-6
EPSL = 1e-5
RSQRT_D = 1.0 / 32.0
MAGIC = 0x5F3759DF
NEWTON_ITERS = 2

DEBUG_TAPS = False


class TK:

    STREAMS = ("sp", "gp", "pe", "act", "dve")

    def __init__(self, nc):
        self.nc = nc
        self.sems = {}
        for s in self.STREAMS:
            self.sems[(s, "cmp")] = nc.alloc_semaphore(f"S_{s}_c")
        self.ch_sems = {}
        self.ch_count = {}
        self.ch_last = {}
        self.ops = []
        self.last_w = {}
        self.readers = {}

    def _record(self, stream, kind, emit, reads, writes, ch=None):
        idx = len(self.ops)
        deps = {}
        for t in reads:
            if t in self.last_w:
                deps[self.last_w[t]] = True
        for t in writes:
            if t in self.last_w:
                deps.setdefault(self.last_w[t], False)
            for r in self.readers.get(t, ()):
                deps.setdefault(r, False)
        o = dict(stream=stream, kind=kind, emit=emit, deps=deps,
                 inc=False, amt=16 if kind == "dma" else 1, ch=ch)
        if kind == "dma":
            if ch not in self.ch_sems:
                self.ch_sems[ch] = self.nc.alloc_semaphore(
                    f"C_{len(self.ch_sems)}")
            if ch in self.ch_last:
                deps.setdefault(self.ch_last[ch], False)
            self.ch_last[ch] = idx
            self.ch_count[ch] = self.ch_count.get(ch, 0) + 1
            o["chidx"] = self.ch_count[ch]
        self.ops.append(o)
        for t in reads:
            self.readers.setdefault(t, []).append(idx)
        for t in writes:
            self.last_w[t] = idx
            self.readers[t] = []
        return idx

    def op(self, stream, method, *args, reads=(), writes=(), **kw):
        def emit(eng, args=args, kw=kw, method=method):
            return getattr(eng, method)(*args, **kw)
        return self._record(stream, "cmp", emit, reads, writes)

    def raw(self, stream, fn, reads=(), writes=()):
        return self._record(stream, "cmp", fn, reads, writes)

    def dma(self, stream, out, in_, reads=(), writes=(), ch=None, **kw):
        def emit(eng, out=out, in_=in_, kw=kw):
            return eng.dma_start(out=out, in_=in_, **kw)
        if ch is None:
            ch = writes[0] if writes else ("misc", stream)
        return self._record(stream, "dma", emit, reads, writes, ch=ch)

    def finalize_and_emit(self, block, engines_map):
        ops = self.ops
        for o in ops:
            best = {}
            for d, is_raw in o["deps"].items():
                p = ops[d]
                if (p["stream"] == o["stream"] and p["kind"] == "cmp"
                        and o["kind"] == "cmp"
                        and (not is_raw or p["stream"] == "pe")):
                    continue
                key = ("ch", p["ch"]) if p["kind"] == "dma" else (p["stream"], "cmp")
                if d > best.get(key, -1):
                    best[key] = d
            o["deps"] = set(best.values())
            for d in o["deps"]:
                ops[d]["inc"] = True
        for o in ops:
            if o["kind"] == "dma":
                o["inc"] = True
        totals = {k: 0 for k in self.sems}
        totals.update({("ch", c): 0 for c in self.ch_sems})
        for o in ops:
            key = ("ch", o["ch"]) if o["kind"] == "dma" else (o["stream"], "cmp")
            if o["inc"]:
                totals[key] += o["amt"]
            o["prefix"] = totals[key]
        self.totals = totals

        def sem_of(key):
            return self.ch_sems[key[1]] if key[0] == "ch" else self.sems[key]

        for s in self.STREAMS:
            s_ops = [o for o in ops if o["stream"] == s]
            if not s_ops:
                continue

            def body(eng, s_ops=s_ops):
                waited = {}
                for o in s_ops:
                    need = {}
                    for d in o["deps"]:
                        p = self.ops[d]
                        key = ("ch", p["ch"]) if p["kind"] == "dma" else (p["stream"], "cmp")
                        v = p["prefix"]
                        if v > waited.get(key, 0):
                            need[key] = max(need.get(key, 0), v)
                    for key, v in need.items():
                        eng.wait_ge(sem_of(key), v)
                        waited[key] = v
                    ins = o["emit"](eng)
                    if o["inc"]:
                        key = (("ch", o["ch"]) if o["kind"] == "dma"
                               else (o["stream"], "cmp"))
                        ins.then_inc(sem_of(key), o["amt"])

            engines_map[s](body)


def _rope_tables():
    i = np.arange(D // 2, dtype=np.float32)
    inv_freq = np.power(np.float32(10000.0),
                        -(i * np.float32(2.0)) / np.float32(D)).astype(np.float32)
    pos = np.arange(N, dtype=np.float32)
    ang = (pos[:, None] * inv_freq[None, :]).astype(np.float32)
    cos = np.cos(ang).astype(np.float32)
    sin = np.sin(ang).astype(np.float32)
    rows = np.arange(128) % 64
    C = cos[:, rows % 32].T.copy()
    Ssig = sin[:, rows % 32].T.copy()
    Ssig[rows >= 32, :] *= -1.0
    return C.astype(np.float32), Ssig.astype(np.float32)


def build_program():
    nc = bass.Bass("TRN2", target_bir_lowering=False, debug=False)

    x_in = nc.dram_tensor("x", [N, E], F32, kind="ExternalInput").ap()
    Wq = nc.dram_tensor("Wq", [E, E], F32, kind="ExternalInput").ap()
    Wk = nc.dram_tensor("Wk", [E, E], F32, kind="ExternalInput").ap()
    Wv = nc.dram_tensor("Wv", [E, E], F32, kind="ExternalInput").ap()
    Wo = nc.dram_tensor("Wo", [E, E], F32, kind="ExternalInput").ap()
    Wg1 = nc.dram_tensor("Wg1", [E, D], F32, kind="ExternalInput").ap()
    Wg2 = nc.dram_tensor("Wg2", [D, E], F32, kind="ExternalInput").ap()
    gamma = nc.dram_tensor("gamma", [E], F32, kind="ExternalInput").ap()
    beta = nc.dram_tensor("beta", [E], F32, kind="ExternalInput").ap()
    r_scale = nc.dram_tensor("r_scale", [E], F32, kind="ExternalInput").ap()
    r_w = nc.dram_tensor("r_w", [E, NEXP], F32, kind="ExternalInput").ap()
    g_w = nc.dram_tensor("g_w", [NEXP, 2, HID, E], F32, kind="ExternalInput").ap()
    l_w = nc.dram_tensor("l_w", [NEXP, HID, E], F32, kind="ExternalInput").ap()
    pe_scale = nc.dram_tensor("pe_scale", [NEXP], F32, kind="ExternalInput").ap()
    y_out = nc.dram_tensor("y", [N, E], F32, kind="ExternalOutput").ap()

    qkv_t = nc.dram_tensor("qkv_t", [24, 128, 8 * 128], BF16, kind="Internal").ap()
    wbf_tmp = nc.dram_tensor("wbf_tmp", [E, E], BF16, kind="Internal").ap()
    gwbf = nc.dram_tensor("gwbf", [NEXP, 2, HID, E], BF16, kind="Internal").ap()
    gwT = nc.dram_tensor("gwT", [NEXP, 2, HID // 128, 128, 8 * 128], BF16,
                         kind="Internal").ap()
    lwbf = nc.dram_tensor("lwbf", [NEXP, HID, E], BF16, kind="Internal").ap()
    wobf = nc.dram_tensor("wobf", [E, E], BF16, kind="Internal").ap()

    C_np, S_np = _rope_tables()
    c_cos = nc.inline_tensor(C_np, name="c_cos").ap()
    c_sin = nc.inline_tensor(S_np, name="c_sin").ap()
    mask_np = (np.arange(128)[:, None] <= np.arange(128)[None, :]).astype(np.float32)
    c_mask = nc.inline_tensor(mask_np, name="c_mask").ap()
    c_idb = nc.inline_tensor(np.eye(128, dtype=ml_dtypes.bfloat16), name="c_idb").ap()
    c_idf = nc.inline_tensor(np.eye(128, dtype=np.float32), name="c_idf").ap()
    c_ones = nc.inline_tensor(np.ones((1, 128), np.float32), name="c_ones").ap()
    sel_np = np.zeros((8, 8 * 128), np.float32)
    for e in range(8):
        sel_np[e, e * 128:(e + 1) * 128] = 1.0
    c_sel = nc.inline_tensor(sel_np, name="c_sel").ap()

    A = nc.alloc_sbuf_tensor

    xn_f = A("xn_f", [128, 8 * TS], BF16)
    acc = A("acc", [128, NCH * E], F32)
    Wg1_sb = A("Wg1_sb", [128, 8 * D], BF16)
    Wg2_sb = A("Wg2_sb", [D, E], BF16)
    rwp_sb = A("rwp_sb", [128, 8 * NEXP], F32)
    wqkv_sl = A("wqkv_sl", [128, 4 * 1024], BF16)
    gs0_sl = A("gs0_sl", [128, 4 * 1024], BF16)
    gs1_sl = A("gs1_sl", [128, 4 * 1024], BF16)
    lw_sl = A("lw_sl", [128, 4 * 1024], BF16)
    act_sb = A("act_sb", [128, 16 * TS], BF16)
    Wo_sb = nc.alloc_sbuf_tensor_at(
        "Wo_sb", [128, 8 * E], BF16, offset=nc.lookup_mloc(act_sb).addr)
    q_ro = A("q_ro", [128, 8 * PG], BF16)
    k_ro = A("k_ro", [128, 8 * PG], BF16)
    v_f = A("v_f", [128, 8 * PG], BF16)
    k_tok = A("k_tok", [128, E], BF16)
    v_tok = A("v_tok", [128, E], BF16)
    qs_f = A("qs_f", [128, PG], F32)
    ks_f = A("ks_f", [128, PG], F32)
    rt_f = A("rt_f", [128, PG], F32)
    rt2_f = A("rt2_f", [128, PG], F32)
    ctab = A("ctab", [128, PG], F32)
    stab = A("stab", [128, PG], F32)
    Fm = A("Fm", [128, H * 128], BF16)
    S_f = A("S_f", [128, 512], F32)
    S_b = A("S_b", [128, 512], BF16)
    g1T = A("g1T", [D, TS], BF16)
    gate_t = A("gate_t", [128, E], F32)
    og = A("og", [128, E], F32)
    z_bf = A("z_bf", [128, E], BF16)
    zT = A("zT", [128, E], BF16)
    sq_scr = A("sq_scr", [128, E], F32)
    gel = A("gel", [128, TS], F32)
    a1 = A("a1", [128, TS], F32)
    c_rep = A("c_rep", [128, TS], F32)
    gam_rep = A("gam_rep", [128, E], F32)
    bWo_rep = A("bWo_rep", [128, E], F32)
    pes_rep = A("pes_rep", [128, NEXP], F32)
    mask_sb = A("mask_sb", [128, 128], F32)
    idb_sb = A("idb_sb", [128, 128], BF16)
    idf_sb = A("idf_sb", [128, 128], F32)
    ones_sb = A("ones_sb", [1, 128], F32)
    sel_sb = A("sel_sb", [8, 8 * 128], F32)
    xin = A("xin", [128, 2 * E], F32)
    out_sb = A("out_sb", [128, 2 * E], F32)
    cT_sb = A("cT_sb", [8, NCH * 128], F32)
    lg = A("lg", [128, NEXP], F32)
    t8a = A("t8a", [128, NEXP], F32)
    t8b = A("t8b", [128, NEXP], F32)
    c_sb = A("c_sb", [128, NEXP], F32)
    col = {k: A(f"col_{k}", [128, 1], F32) for k in
           ("sum", "sum2", "m", "rstd", "rstd1", "v", "t1", "t2", "m2", "r")}
    coli = A("coli", [128, 1], I32)
    rs8 = A("rs8", [8, 128], F32)
    rsT = A("rsT", [128, 8], F32)
    row_f = A("row_f", [1, E], F32)
    ph0 = A("ph0", [128, 16384], BF16)

    PS = nc.alloc_psum_tensor("PS", [128, 4096], F32)

    def bank(b, lo=0, hi=512):
        return PS[:, b * 512 + lo: b * 512 + hi]

    def bank_bf(b, lo=0, n=64):
        return PS[:, b * 512 + lo: b * 512 + lo + n].bitcast(BF16)

    tk = TK(nc)
    SP, GP, PE, ACT, DVE = "sp", "gp", "pe", "act", "dve"

    taps = {}

    def tap(name, ap, reads):
        if not DEBUG_TAPS:
            return
        t = nc.dram_tensor("tap_" + name, list(ap.shape), ap.dtype,
                           kind="ExternalOutput").ap()
        taps[name] = t
        tk.dma(GP, t, ap, reads=reads)

    tk.dma(GP, mask_sb[:, :], c_mask[:, :], writes=[("mask",)], ch="init")
    tk.dma(GP, idb_sb[:, :], c_idb[:, :], writes=[("idb",)], ch="init")
    tk.dma(GP, idf_sb[:, :], c_idf[:, :], writes=[("idf",)], ch="init")
    tk.dma(GP, ones_sb[:, :], c_ones[:, :], writes=[("ones",)], ch="init")
    tk.dma(GP, sel_sb[:, :], c_sel[:, :], writes=[("sel",)], ch="init")

    PH0R = lambda lo, hi: [("ph0", r) for r in range(lo, hi)]
    for m, W in enumerate((Wq, Wk, Wv)):
        tk.dma(GP, wbf_tmp[:, :], W[:, :], writes=[("wbft",)])
        wall = ph0[:, 0:8192].rearrange("p (dc o) -> p dc o", dc=8)
        tk.dma(GP, wall,
               wbf_tmp.rearrange("(dc p) o -> p dc o", p=128),
               reads=[("wbft",)], writes=PH0R(0, 4))
        for oc in range(8):
            tk.dma(SP, qkv_t[m * 8 + oc, :, :].rearrange("p (dc o) -> p dc o", dc=8),
                   wall[:, :, oc * 128:(oc + 1) * 128],
                   reads=PH0R(0, 4), writes=[("qkvt", m, oc)], ch="qkvt")
    tk.dma(GP, wobf[:, :], Wo[:, :], writes=[("wobf",)], ch="init")

    def emit_wo_load():
        for dc in range(8):
            tk.dma(SP, Wo_sb[:, dc * E:(dc + 1) * E], wobf[dc * 128:(dc + 1) * 128, :],
                   reads=[("wobf",)],
                   writes=[("wo",)] + [("actk", hh) for hh in range(HID // 128)], ch="wo")
    emit_wo_load()
    for dc in range(8):
        tk.dma(GP, Wg1_sb[:, dc * D:(dc + 1) * D], Wg1[dc * 128:(dc + 1) * 128, :],
               writes=[("wg1",)], ch="init")
    tk.dma(GP, Wg2_sb[:, :], Wg2[:, :], writes=[("wg2",)], ch="init")

    def emit_conv_casts(e):
        for i in range(2):
            tk.dma(GP, gwbf[e, i, :, :], g_w[e, i, :, :], writes=[("gwbf", e, i)], ch="gwbf")
        tk.dma(GP, lwbf[e, :, :], l_w[e, :, :], writes=[("lwbf", e)], ch="lwbf")

    def emit_conv_expert(e):
        for i in range(2):
            for dc in range(8):
                tk.dma(SP, ph0[:, dc * 2048:(dc + 1) * 2048],
                       gwbf[e, i, :, dc * 128:(dc + 1) * 128],
                       transpose=True, reads=[("gwbf", e, i)],
                       writes=[("ph0", dc)])
            ph0v = ph0[:, :].rearrange("p (dc h) -> p dc h", dc=8)
            for hc in range(HID // 128):
                tk.dma(SP, gwT[e, i, hc, :, :].rearrange("p (dc h) -> p dc h", dc=8),
                       ph0v[:, :, hc * 128:(hc + 1) * 128],
                       reads=PH0R(0, 8), writes=[("gwT", e, i, hc)],
                       ch=("gwTw", hc % 4))

    tk.dma(GP, rs8[:, :], r_scale.rearrange("(dc p) -> dc p", p=128), writes=[("rs8",)], ch="init")
    tk.op(PE, "transpose", bank(1, 0, 8), rs8[:, :], idf_sb[0:8, 0:8],
          reads=[("rs8",), ("idf",)], writes=[("ps", 1)])
    tk.op(DVE, "tensor_copy", rsT[:, :], bank(1, 0, 8),
          reads=[("ps", 1)], writes=[("rsT",)])
    for dc in range(8):
        tk.dma(GP, lg[:, :], r_w[dc * 128:(dc + 1) * 128, :], writes=[("lg",)], ch="init")
        tk.op(DVE, "scalar_tensor_tensor", rwp_sb[:, dc * NEXP:(dc + 1) * NEXP],
              lg[:, :], RSQRT_D,
              rsT[:, dc:dc + 1].broadcast_to((128, NEXP)), ALU.mult, ALU.mult,
              reads=[("lg",), ("rsT",)], writes=[("rwp",)])

    tk.dma(GP, row_f[:, :], gamma.unsqueeze(0), writes=[("rowf",)], ch="init")
    for j in range(2):
        tk.op(PE, "matmul", bank(2 + j), ones_sb[:, :], row_f[:, j * 512:(j + 1) * 512],
              start=True, stop=True, reads=[("rowf",), ("ones",)], writes=[("ps", 2 + j)])
    tk.op(DVE, "tensor_copy", gam_rep[:, :], PS[:, 2 * 512:4 * 512],
          reads=[("ps", 2), ("ps", 3)], writes=[("gam",)])
    tk.dma(GP, row_f[0:1, 0:NEXP], pe_scale.unsqueeze(0), writes=[("rowf",)], ch="init")
    tk.op(PE, "matmul", bank(2, 0, NEXP), ones_sb[:, :], row_f[0:1, 0:NEXP],
          start=True, stop=True, reads=[("rowf",), ("ones",)], writes=[("ps", 2)])
    tk.op(DVE, "tensor_copy", pes_rep[:, :], bank(2, 0, NEXP),
          reads=[("ps", 2)], writes=[("pes",)])
    tk.dma(GP, rs8[:, :], beta.rearrange("(dc p) -> dc p", p=128), writes=[("rs8",)], ch="init")
    tk.op(PE, "transpose", bank(1, 0, 8), rs8[:, :], idf_sb[0:8, 0:8],
          reads=[("rs8",), ("idf",)], writes=[("ps", 1)])
    tk.op(DVE, "tensor_copy", rsT[:, :], bank(1, 0, 8),
          reads=[("ps", 1)], writes=[("rsT",)])
    tk.op(DVE, "tensor_copy", z_bf[:, 0:8], rsT[:, :], reads=[("rsT",)],
          writes=[("znb",)])
    for j in range(2):
        for dc in range(8):
            tk.op(PE, "matmul", PS[0:1, (4 + j) * 512:(5 + j) * 512],
                  z_bf[:, dc:dc + 1],
                  Wo_sb[:, dc * E + j * 512: dc * E + (j + 1) * 512],
                  start=(dc == 0), stop=(dc == 7),
                  reads=[("znb",), ("wo",)], writes=[("ps", 4 + j)])
    tk.op(DVE, "tensor_copy", row_f[:, :], PS[0:1, 4 * 512:6 * 512],
          reads=[("ps", 4), ("ps", 5)], writes=[("rowf",)])
    for j in range(2):
        tk.op(PE, "matmul", bank(2 + j), ones_sb[:, :], row_f[:, j * 512:(j + 1) * 512],
              start=True, stop=True, reads=[("rowf",), ("ones",)], writes=[("ps", 2 + j)])
    tk.op(DVE, "tensor_copy", bWo_rep[:, :], PS[:, 2 * 512:4 * 512],
          reads=[("ps", 2), ("ps", 3)], writes=[("bwo",)])

    def newton_rsqrt(vcol, outcol, tags_r, tag_w):
        tk.op(DVE, "tensor_scalar", coli[:, :], vcol.bitcast(I32), 1, None,
              ALU.arith_shift_right, reads=tags_r, writes=[("coli",)])
        tk.op(DVE, "tensor_scalar", coli[:, :], coli[:, :], -1, MAGIC,
              ALU.mult, ALU.add, reads=[("coli",)], writes=[("coli",)])
        for it in range(NEWTON_ITERS):
            tk.op(DVE, "tensor_mul", col["t1"][:, :], coli[:, :].bitcast(F32),
                  coli[:, :].bitcast(F32), reads=[("coli",)], writes=[("c_t1",)])
            tk.op(DVE, "tensor_mul", col["t1"][:, :], col["t1"][:, :], vcol,
                  reads=[("c_t1",)] + list(tags_r), writes=[("c_t1",)])
            tk.op(DVE, "tensor_scalar", col["t1"][:, :], col["t1"][:, :], -0.5, 1.5,
                  ALU.mult, ALU.add, reads=[("c_t1",)], writes=[("c_t1",)])
            last = (it == NEWTON_ITERS - 1)
            tk.op(DVE, "tensor_tensor",
                  outcol if last else coli[:, :].bitcast(F32),
                  coli[:, :].bitcast(F32), col["t1"][:, :], ALU.mult,
                  reads=[("coli",), ("c_t1",)],
                  writes=[tag_w if last else ("coli",)])

    tap("rwp", rwp_sb[:, :], [("rwp",)])
    tap("rowf", row_f[:, :], [("rowf",)])
    tap("rsT2", rsT[:, :], [("rsT",)])
    tap("zb8", z_bf[:, 0:8], [("znb",)])
    tap("gam", gam_rep[:, :], [("gam",)])
    tap("bwo", bWo_rep[:, :], [("bwo",)])
    tap("pes", pes_rep[:, :], [("pes",)])
    tap("wo0", Wo_sb[:, 0:E], [("wo",)])
    tap("qkvt0", qkv_t[0, :, :], [("qkvt", 0, 0)])
    tap("qkvt8", qkv_t[8, :, :], [("qkvt", 1, 0)])
    tap("gwt0", gwT[0, 0, 0, :, :], [("gwT", 0, 0, 0)])
    tap("lwbf0", lwbf[0, 0:128, :], [("lwbf", 0)])

    tk.op(DVE, "memset", S_f[:, :], 0.0, writes=[("sf",)])
    tk.op(DVE, "memset", S_b[:, :], 0.0, writes=[("sb",)])

    def emit_passA(st):
        t0 = st * TS

        for i in range(NCH):
            par = i % 2
            xv = xin[:, par * E:(par + 1) * E]
            tk.dma(ACT, xv, x_in[t0 + i * CH: t0 + (i + 1) * CH, :],
                   writes=[("xin", par)])
            tk.op(DVE, "reduce_sum", col["sum"][:, :], xv, axis=AX.X,
                  reads=[("xin", par)], writes=[("csum",)])
            tk.op(ACT, "activation", sq_scr[:, :], xv, AF.Square,
                  accum_out=col["sum2"][:, :],
                  reads=[("xin", par)], writes=[("sqscr",), ("csum2",)])
            tk.op(DVE, "tensor_scalar_mul", col["m"][:, :], col["sum"][:, :],
                  1.0 / E, reads=[("csum",)], writes=[("cm",)])
            tk.op(DVE, "tensor_mul", col["t2"][:, :], col["m"][:, :], col["m"][:, :],
                  reads=[("cm",)], writes=[("ct2",)])
            tk.op(DVE, "scalar_tensor_tensor", col["v"][:, :], col["sum2"][:, :],
                  1.0 / E, col["t2"][:, :], ALU.mult, ALU.subtract,
                  reads=[("csum2",), ("ct2",)], writes=[("cv",)])
            tk.op(DVE, "tensor_scalar_add", col["v"][:, :], col["v"][:, :], EPS1,
                  reads=[("cv",)], writes=[("cv",)])
            newton_rsqrt(col["v"][:, :], col["rstd"][:, :], [("cv",)], ("crstd", 0))
            tk.op(DVE, "tensor_scalar", sq_scr[:, :], xv, col["m"][:, :],
                  col["rstd"][:, :], ALU.subtract, ALU.mult,
                  reads=[("xin", par), ("cm",), ("crstd", 0)], writes=[("sqscr",)])
            tk.op(ACT, "activation", z_bf[:, :], sq_scr[:, :], AF.Copy,
                  reads=[("sqscr",)], writes=[("znb",)])
            for dc in range(8):
                tk.op(PE, "transpose", bank_bf(7, dc * 64, 64),
                      z_bf[:, dc * 128:(dc + 1) * 128], idb_sb[:, :],
                      reads=[("znb",), ("idb",)], writes=[("ps", 7)])
            for dc in range(8):
                tk.op(DVE, "tensor_copy",
                      xn_f[:, dc * TS + i * CH: dc * TS + (i + 1) * CH],
                      bank_bf(7, dc * 64, 64),
                      reads=[("ps", 7)], writes=[("xnf", dc, i)])
            tk.op(DVE, "reciprocal", col["t1"][:, :], col["v"][:, :],
                  reads=[("cv",)], writes=[("c_t1",)])
            tk.op(DVE, "tensor_scalar", col["t2"][:, :], col["t1"][:, :], -EPS1,
                  1.0, ALU.mult, ALU.add, reads=[("c_t1",)], writes=[("ct2",)])
            tk.op(DVE, "tensor_scalar_add", col["t2"][:, :], col["t2"][:, :], EPS2,
                  reads=[("ct2",)], writes=[("ct2",)])
            newton_rsqrt(col["t2"][:, :], col["r"][:, :], [("ct2",)], ("c_r",))
            for half in range(2):
                for s4 in range(4):
                    dc = half * 4 + s4
                    tk.op(PE, "transpose",
                          PS[:, 7 * 512 + s4 * 128: 7 * 512 + (s4 + 1) * 128],
                          sq_scr[:, dc * 128:(dc + 1) * 128], idf_sb[:, :],
                          reads=[("sqscr",), ("idf",)], writes=[("ps", 7)])
                for s4 in range(4):
                    dc = half * 4 + s4
                    tk.op(DVE, "tensor_copy", og[:, dc * 128:(dc + 1) * 128],
                          PS[:, 7 * 512 + s4 * 128: 7 * 512 + (s4 + 1) * 128],
                          reads=[("ps", 7)], writes=[("og",)])
            for dc in range(8):
                tk.op(PE, "matmul", bank(0, 0, NEXP),
                      og[:, dc * 128:(dc + 1) * 128],
                      rwp_sb[:, dc * NEXP:(dc + 1) * NEXP],
                      start=(dc == 0), stop=(dc == 7),
                      reads=[("og",), ("rwp",)], writes=[("ps", 0)])
            tk.op(DVE, "tensor_scalar_mul", lg[:, :], bank(0, 0, NEXP),
                  col["r"][:, :], reads=[("ps", 0), ("c_r",)], writes=[("lg",)])
            tk.op(DVE, "reduce_max", col["m"][:, :], lg[:, :], axis=AX.X,
                  reads=[("lg",)], writes=[("cm",)])
            tk.op(DVE, "tensor_scalar", t8a[:, :], lg[:, :], col["m"][:, :], None,
                  ALU.is_ge, reads=[("lg",), ("cm",)], writes=[("t8a",)])
            tk.op(DVE, "scalar_tensor_tensor", t8b[:, :], t8a[:, :], -1e30,
                  lg[:, :], ALU.mult, ALU.add, reads=[("t8a",), ("lg",)],
                  writes=[("t8b",)])
            tk.op(DVE, "reduce_max", col["m2"][:, :], t8b[:, :], axis=AX.X,
                  reads=[("t8b",)], writes=[("cm2",)])
            tk.op(DVE, "tensor_scalar_mul", col["t1"][:, :], col["m"][:, :], -1.0,
                  reads=[("cm",)], writes=[("c_t1",)])
            tk.op(ACT, "activation", col["t2"][:, :], col["m2"][:, :], AF.Exp,
                  bias=col["t1"][:, :], scale=1.0,
                  reads=[("cm2",), ("c_t1",)], writes=[("ct2",)])
            tk.op(DVE, "tensor_scalar_add", col["t2"][:, :], col["t2"][:, :], 1.0,
                  reads=[("ct2",)], writes=[("ct2",)])
            tk.op(DVE, "reciprocal", col["r"][:, :], col["t2"][:, :],
                  reads=[("ct2",)], writes=[("c_r",)])
            tk.op(ACT, "activation", t8a[:, :], lg[:, :], AF.Exp,
                  bias=col["t1"][:, :], scale=1.0,
                  reads=[("lg",), ("c_t1",)], writes=[("t8a",)])
            tk.op(DVE, "tensor_scalar", t8b[:, :], lg[:, :], col["m2"][:, :], None,
                  ALU.is_ge, reads=[("lg",), ("cm2",)], writes=[("t8b",)])
            tk.op(DVE, "tensor_mul", t8a[:, :], t8a[:, :], t8b[:, :],
                  reads=[("t8a",), ("t8b",)], writes=[("t8a",)])
            tk.op(DVE, "tensor_scalar_mul", t8a[:, :], t8a[:, :], col["r"][:, :],
                  reads=[("t8a",), ("c_r",)], writes=[("t8a",)])
            tk.op(DVE, "tensor_mul", c_sb[:, :], t8a[:, :], pes_rep[:, :],
                  reads=[("t8a",), ("pes",)], writes=[("csb",)])
            tk.op(PE, "transpose", PS[0:NEXP, 1 * 512:1 * 512 + 128], c_sb[:, :],
                  idf_sb[:, :], reads=[("csb",), ("idf",)], writes=[("ps", 1)])
            tk.op(DVE, "tensor_copy", cT_sb[:, i * 128:(i + 1) * 128],
                  PS[0:NEXP, 1 * 512:1 * 512 + 128],
                  reads=[("ps", 1)], writes=[("cT", i)])

        if st == 0:
            tap("xnf", xn_f[:, :],
                [("xnf", dc, ii) for dc in range(8) for ii in range(NCH)])
            tap("ct", cT_sb[:, :], [("cT", ii) for ii in range(NCH)])

    def emit_attn(st):
        t0 = st * TS
        pend = [None]
        if st >= 1:
            emit_wo_load()
        for dc in range(8):
            tk.op(PE, "matmul", PS[0:D, 0:TS], Wg1_sb[:, dc * D:(dc + 1) * D],
                  xn_f[:, dc * TS:(dc + 1) * TS], start=(dc == 0), stop=(dc == 7),
                  reads=[("wg1",)] + [("xnf", dc, ii) for ii in range(NCH)],
                  writes=[("ps", 0)])
        tk.op(DVE, "tensor_copy", g1T[:, :], PS[0:D, 0:TS],
              reads=[("ps", 0)], writes=[("g1T",)])

        for g in range(NPG):
            gt0 = t0 + g * PG
            tk.dma(ACT, ctab[:, :], c_cos[:, gt0:gt0 + PG], writes=[("ctab",)])
            tk.dma(ACT, stab[:, :], c_sin[:, gt0:gt0 + PG], writes=[("stab",)])
            for oc in range(24):
                par = oc % 4
                sl = wqkv_sl[:, par * 1024:(par + 1) * 1024]
                tk.dma(SP, sl, qkv_t[oc, :, :], writes=[("wsl", par)])
                for dc in range(8):
                    tk.op(PE, "matmul", bank(par, 0, PG),
                          sl[:, dc * 128:(dc + 1) * 128],
                          xn_f[:, dc * TS + g * PG: dc * TS + (g + 1) * PG],
                          start=(dc == 0), stop=(dc == 7),
                          reads=[("wsl", par)] +
                                [("xnf", dc, g * 2), ("xnf", dc, g * 2 + 1)],
                          writes=[("ps", par)])
                m, fc = divmod(oc, 8)
                if m < 2:
                    s_out = qs_f if m == 0 else ks_f
                    s_tag = ("qs",) if m == 0 else ("ks",)
                    dst = q_ro if m == 0 else k_ro
                    d_tag = ("qro", fc) if m == 0 else ("kro", fc)
                    tk.op(ACT, "activation", s_out[:, :], bank(par, 0, PG),
                          AF.Silu, reads=[("ps", par)], writes=[s_tag])
                    tk.op(DVE, "tensor_mul", rt_f[:, :], s_out[:, :], ctab[:, :],
                          reads=[s_tag, ("ctab",)], writes=[("rt",)])
                    for n_, (o_lo, i_lo) in enumerate(
                            ((0, 32), (32, 0), (64, 96), (96, 64))):
                        tk.op(DVE if n_ < 2 else GP, "tensor_mul",
                              rt2_f[o_lo:o_lo + 32, :],
                              s_out[i_lo:i_lo + 32, :], stab[i_lo:i_lo + 32, :],
                              reads=[s_tag, ("stab",)], writes=[("rt2",)])
                    tk.op(GP, "tensor_add", dst[:, fc * PG:(fc + 1) * PG],
                          rt_f[:, :], rt2_f[:, :],
                          reads=[("rt",), ("rt2",)], writes=[d_tag])
                else:
                    tk.op(ACT, "activation", v_f[:, fc * PG:(fc + 1) * PG],
                          bank(par, 0, PG), AF.Copy,
                          reads=[("ps", par)], writes=[("vf", fc)])

            if st == 0 and g == 0:
                tap("qro", q_ro[:, :], [("qro", fc) for fc in range(8)])
                tap("kro", k_ro[:, :], [("kro", fc) for fc in range(8)])
                tap("vfm", v_f[:, :], [("vf", fc) for fc in range(8)])
                tap("g1t", g1T[:, :], [("g1T",)])

            for cc in range(2):
                i = g * 2 + cc
                for fc in range(8):
                    tk.op(PE, "transpose", bank_bf(7, fc * 64, 64),
                          k_ro[:, fc * PG + cc * 128: fc * PG + (cc + 1) * 128],
                          idb_sb[:, :],
                          reads=[("kro", fc), ("idb",)], writes=[("ps", 7)])
                for fc in range(8):
                    tk.op(ACT, "activation", k_tok[:, fc * 128:(fc + 1) * 128],
                          bank_bf(7, fc * 64, 64), AF.Copy,
                          reads=[("ps", 7)], writes=[("ktok", fc)])
                for fc in range(8):
                    tk.op(PE, "transpose", bank_bf(7, fc * 64, 64),
                          v_f[:, fc * PG + cc * 128: fc * PG + (cc + 1) * 128],
                          idb_sb[:, :],
                          reads=[("vf", fc), ("idb",)], writes=[("ps", 7)])
                for fc in range(8):
                    tk.op(ACT, "activation", v_tok[:, fc * 128:(fc + 1) * 128],
                          bank_bf(7, fc * 64, 64), AF.Copy,
                          reads=[("ps", 7)], writes=[("vtok", fc)])
                for h in range(H):
                    fc, half = divmod(h, 2)
                    rows = slice(half * 64, half * 64 + 64)
                    fb = 2 + h % 2
                    tk.op(PE, "matmul", bank(fb, 0, 128),
                          k_ro[rows, fc * PG + cc * 128: fc * PG + (cc + 1) * 128],
                          q_ro[rows, fc * PG + cc * 128: fc * PG + (cc + 1) * 128],
                          start=True, stop=True,
                          reads=[("kro", fc), ("qro", fc)], writes=[("ps", fb)])
                    tk.op(DVE, "tensor_mul", Fm[:, h * 128:(h + 1) * 128],
                          bank(fb, 0, 128), mask_sb[:, :],
                          reads=[("ps", fb), ("mask",)], writes=[("fm", h)])
                for j in range(2):
                    tk.op(PE, "matmul", bank(j, 0, 512),
                          g1T[:, i * 128:(i + 1) * 128],
                          Wg2_sb[:, j * 512:(j + 1) * 512],
                          start=True, stop=True,
                          reads=[("g1T",), ("wg2",)], writes=[("ps", j)])
                tk.op(ACT, "activation", gate_t[:, :], PS[:, 0:1024], AF.Sigmoid,
                      reads=[("ps", 0), ("ps", 1)], writes=[("gatet",)])
                for h in range(H):
                    fc, half = divmod(h, 2)
                    rows = slice(half * 64, half * 64 + 64)
                    ob, olo = divmod(h * 64, 512)
                    tk.op(PE, "matmul", bank(4 + ob, olo, olo + 64),
                          Fm[:, h * 128:(h + 1) * 128], v_tok[:, h * 64:(h + 1) * 64],
                          start=True, stop=False,
                          reads=[("fm", h), ("vtok", h // 2)], writes=[("ps", 4 + ob)])
                    tk.op(PE, "matmul", bank(4 + ob, olo, olo + 64),
                          q_ro[rows, fc * PG + cc * 128: fc * PG + (cc + 1) * 128],
                          S_b[rows, (h // 2) * 64:(h // 2) * 64 + 64],
                          start=False, stop=True,
                          reads=[("qro", fc), ("sb",)], writes=[("ps", 4 + ob)])
                for h in range(H):
                    half = h % 2
                    tk.op(PE, "matmul",
                          PS[half * 64:half * 64 + 64,
                             6 * 512 + (h // 2) * 64: 6 * 512 + (h // 2) * 64 + 64],
                          k_tok[:, h * 64:(h + 1) * 64], v_tok[:, h * 64:(h + 1) * 64],
                          start=(h < 2), stop=(h >= H - 2),
                          reads=[("ktok", h // 2), ("vtok", h // 2)],
                          writes=[("ps", 6)])
                tk.op(DVE, "tensor_add", S_f[:, :], S_f[:, :], bank(6),
                      reads=[("ps", 6), ("sf",)], writes=[("sf",)])
                tk.op(ACT, "activation", S_b[:, :], S_f[:, :], AF.Copy,
                      reads=[("sf",)], writes=[("sb",)])
                tk.op(DVE, "tensor_mul", og[:, :], PS[:, 4 * 512:6 * 512],
                      gate_t[:, :],
                      reads=[("ps", 4), ("ps", 5), ("gatet",)], writes=[("og",)])
                tk.op(DVE, "reduce_sum", col["sum"][:, :], og[:, :], axis=AX.X,
                      reads=[("og",)], writes=[("csum",)])
                tk.op(ACT, "activation", sq_scr[:, :], og[:, :], AF.Square,
                      accum_out=col["sum2"][:, :],
                      reads=[("og",)], writes=[("sqscr",), ("csum2",)])
                rcol = col["rstd"] if i % 2 == 0 else col["rstd1"]
                rtag = ("crstd", i % 2)

                def s1b(i=i, rcol=rcol, rtag=rtag):
                    tk.op(DVE, "tensor_scalar_mul", col["m"][:, :], col["sum"][:, :],
                          1.0 / E, reads=[("csum",)], writes=[("cm",)])
                    tk.op(DVE, "tensor_mul", col["t2"][:, :], col["m"][:, :],
                          col["m"][:, :], reads=[("cm",)], writes=[("ct2",)])
                    tk.op(DVE, "scalar_tensor_tensor", col["v"][:, :], col["sum2"][:, :],
                          1.0 / E, col["t2"][:, :], ALU.mult, ALU.subtract,
                          reads=[("csum2",), ("ct2",)], writes=[("cv",)])
                    tk.op(DVE, "tensor_scalar_add", col["v"][:, :], col["v"][:, :], EPSL,
                          reads=[("cv",)], writes=[("cv",)])
                    newton_rsqrt(col["v"][:, :], rcol[:, :], [("cv",)], rtag)
                    tk.op(DVE, "scalar_tensor_tensor", z_bf[:, :], og[:, :],
                          col["m"][:, :], gam_rep[:, :], ALU.subtract, ALU.mult,
                          reads=[("og",), ("cm",), ("gam",)], writes=[("znb",)])

                if pend[0] is not None:
                    pend[0]()
                s1b()
                if st == 0 and i == 0:
                    tap("ktok0", k_tok[:, :], [("ktok", fc) for fc in range(8)])
                    tap("vtok0", v_tok[:, :], [("vtok", fc) for fc in range(8)])
                    tap("fm0", Fm[:, :], [("fm", h) for h in range(H)])
                    tap("sf0", S_f[:, :], [("sf",)])
                    tap("gatet0", gate_t[:, :], [("gatet",)])
                    tap("og0", og[:, :], [("og",)])
                    tap("zbf0", z_bf[:, :], [("znb",)])

                def epiB(i=i, rcol=rcol, rtag=rtag):
                    for dc in range(8):
                        tk.op(PE, "transpose", bank_bf(7, dc * 64, 64),
                              z_bf[:, dc * 128:(dc + 1) * 128], idb_sb[:, :],
                              reads=[("znb",), ("idb",)], writes=[("ps", 7)])
                    for dc in range(8):
                        tk.op(ACT, "activation", zT[:, dc * 128:(dc + 1) * 128],
                              bank_bf(7, dc * 64, 64), AF.Copy,
                              reads=[("ps", 7)], writes=[("zT", dc)])
                    for jh in range(2):
                        for dc in range(8):
                            tk.op(PE, "matmul", bank(jh, 0, 512),
                                  zT[:, dc * 128:(dc + 1) * 128],
                                  Wo_sb[:, dc * E + jh * 512: dc * E + (jh + 1) * 512],
                                  start=(dc == 0), stop=(dc == 7),
                                  reads=[("zT", dc), ("wo",)], writes=[("ps", jh)])
                    tk.op(DVE, "scalar_tensor_tensor", acc[:, i * E:(i + 1) * E],
                          PS[:, 0:1024], rcol[:, :], bWo_rep[:, :],
                          ALU.mult, ALU.add,
                          reads=[("ps", 0), ("ps", 1), rtag, ("bwo",)],
                          writes=[("acc", i)])
                    if st == 0 and i == 0:
                        tap("acc0", acc[:, 0:E], [("acc", 0)])

                pend[0] = epiB
        pend[0]()
        pend[0] = None
        tap(f"sfst{st}", S_f[:, :], [("sf",)])
        tap(f"xnfst{st}", xn_f[:, 0:512], [("xnf", 0, ii) for ii in range(NCH)])

    def emit_ffn_expert(st, e):
        if True:
            tk.op(PE, "matmul", bank(6, 0, 512), sel_sb[:, e * 128:(e + 1) * 128],
                  cT_sb[:, :], start=True, stop=True,
                  reads=[("sel",)] + [("cT", ii) for ii in range(NCH)],
                  writes=[("ps", 6)])
            tk.op(DVE, "tensor_copy", c_rep[:, :], bank(6, 0, 512),
                  reads=[("ps", 6)], writes=[("crep",)])
            for hc in range(HID // 128):
                par = hc % 4
                g0b, g1b = (hc % 2) * 2, (hc % 2) * 2 + 1
                s0 = gs0_sl[:, par * 1024:(par + 1) * 1024]
                s1 = gs1_sl[:, par * 1024:(par + 1) * 1024]
                tk.dma(SP, s0, gwT[e, 0, hc, :, :], writes=[("gs0", par)])
                tk.dma(SP, s1, gwT[e, 1, hc, :, :], writes=[("gs1", par)])
                for dc in range(8):
                    tk.op(PE, "matmul", bank(g0b), s0[:, dc * 128:(dc + 1) * 128],
                          xn_f[:, dc * TS:(dc + 1) * TS],
                          start=(dc == 0), stop=(dc == 7),
                          reads=[("gs0", par)] + [("xnf", dc, ii) for ii in range(NCH)],
                          writes=[("ps", g0b)])
                for dc in range(8):
                    tk.op(PE, "matmul", bank(g1b), s1[:, dc * 128:(dc + 1) * 128],
                          xn_f[:, dc * TS:(dc + 1) * TS],
                          start=(dc == 0), stop=(dc == 7),
                          reads=[("gs1", par)] + [("xnf", dc, ii) for ii in range(NCH)],
                          writes=[("ps", g1b)])
                tk.op(ACT, "activation", gel[:, :], bank(g0b), AF.Gelu_apprx_tanh,
                      reads=[("ps", g0b)], writes=[("gel",)])
                tk.op(DVE, "tensor_mul", a1[:, :], gel[:, :], bank(g1b),
                      reads=[("gel",), ("ps", g1b)], writes=[("a1",)])
                tk.op(GP, "tensor_tensor", act_sb[:, hc * TS:(hc + 1) * TS],
                      a1[:, :], c_rep[:, :], ALU.mult,
                      reads=[("a1",), ("crep",)], writes=[("actk", hc), ("wo",)])
            for hc in range(HID // 128):
                par = hc % 4
                sl = lw_sl[:, par * 1024:(par + 1) * 1024]
                tk.dma(SP, sl, lwbf[e, hc * 128:(hc + 1) * 128, :],
                       writes=[("lwsl", par)])
                for isl in range(4):
                    for jh in range(2):
                        b = isl * 2 + jh
                        tk.op(PE, "matmul", bank(b),
                              act_sb[:, hc * TS + isl * 128: hc * TS + (isl + 1) * 128],
                              sl[:, jh * 512:(jh + 1) * 512],
                              start=(hc == 0), stop=(hc == HID // 128 - 1),
                              reads=[("actk", hc), ("lwsl", par)],
                              writes=[("ps", b)])
            for isl in range(4):
                tk.op(DVE, "tensor_add", acc[:, isl * E:(isl + 1) * E],
                      acc[:, isl * E:(isl + 1) * E], PS[:, isl * 1024:(isl + 1) * 1024],
                      reads=[("ps", isl * 2), ("ps", isl * 2 + 1), ("acc", isl)],
                      writes=[("acc", isl)])
            if st == 0 and e == 0:
                tap("actsb0", act_sb[:, :],
                    [("actk", hh) for hh in range(HID // 128)])
                tap("crep0", c_rep[:, :], [("crep",)])
                tap("accf0", acc[:, 0:E], [("acc", 0)])
            if st == 0 and e == 7:
                tap("actsb7", act_sb[:, :],
                    [("actk", hh) for hh in range(HID // 128)])
                tap("crep7", c_rep[:, :], [("crep",)])
                tap("accful", acc[:, :], [("acc", ii) for ii in range(NCH)])
            if e == 7:
                tap(f"accst{st}", acc[:, :], [("acc", ii) for ii in range(NCH)])

    def emit_store(st):
        t0 = st * TS
        for i in range(NCH):
            par = i % 2
            ov = out_sb[:, par * E:(par + 1) * E]
            tk.dma(ACT, ov, x_in[t0 + i * CH: t0 + (i + 1) * CH, :],
                   writes=[("outs", par)])
            tk.op(DVE, "tensor_add", ov, acc[:, i * E:(i + 1) * E], ov,
                  reads=[("outs", par), ("acc", i)], writes=[("outs", par)])
            tk.dma(ACT, y_out[t0 + i * CH: t0 + (i + 1) * CH, :], ov,
                   reads=[("outs", par)], writes=[("y", st, i)], ch="y")

    emit_passA(0)
    for e in range(NEXP):
        emit_conv_casts(e)
    emit_attn(0)
    emit_conv_expert(0)
    emit_conv_expert(1)
    for e in range(NEXP):
        if e + 2 < NEXP:
            emit_conv_expert(e + 2)
        emit_ffn_expert(0, e)
    for st in range(1, NST):
        emit_passA(st)
        emit_store(st - 1)
        emit_attn(st)
        for e in range(NEXP):
            emit_ffn_expert(st, e)
    emit_store(NST - 1)

    with nc.Block() as block:
        engines_map = dict(sp=block.sync, gp=block.gpsimd, pe=block.tensor,
                           act=block.scalar, dve=block.vector)
        tk.finalize_and_emit(block, engines_map)
        def final_sync(eng):
            for key, tot in tk.totals.items():
                if tot > 0:
                    sem = tk.ch_sems[key[1]] if key[0] == "ch" else tk.sems[key]
                    eng.wait_ge(sem, tot)
        block.sync(final_sync)

    return nc


_S_INIT_FIXED = False


def kernel(L0_attn_x, L0_attn_Wq, L0_attn_Wk, L0_attn_Wv, L0_attn_Wo,
           L0_attn_Wg1, L0_attn_Wg2, L0_attn_gamma, L0_attn_beta,
           L0_ffn_router_scale, L0_ffn_router_w, L0_ffn_gating_w,
           L0_ffn_linear_w, L0_ffn_per_expert_scale):
    x = np.ascontiguousarray(np.asarray(L0_attn_x, dtype=np.float32))
    common = {
        "Wq": np.ascontiguousarray(np.asarray(L0_attn_Wq, np.float32)),
        "Wk": np.ascontiguousarray(np.asarray(L0_attn_Wk, np.float32)),
        "Wv": np.ascontiguousarray(np.asarray(L0_attn_Wv, np.float32)),
        "Wo": np.ascontiguousarray(np.asarray(L0_attn_Wo, np.float32)),
        "Wg1": np.ascontiguousarray(np.asarray(L0_attn_Wg1, np.float32)),
        "Wg2": np.ascontiguousarray(np.asarray(L0_attn_Wg2, np.float32)),
        "gamma": np.ascontiguousarray(np.asarray(L0_attn_gamma, np.float32)),
        "beta": np.ascontiguousarray(np.asarray(L0_attn_beta, np.float32)),
        "r_scale": np.ascontiguousarray(np.asarray(L0_ffn_router_scale, np.float32)),
        "r_w": np.ascontiguousarray(np.asarray(L0_ffn_router_w, np.float32)),
        "g_w": np.ascontiguousarray(np.asarray(L0_ffn_gating_w, np.float32)),
        "l_w": np.ascontiguousarray(np.asarray(L0_ffn_linear_w, np.float32)),
        "pe_scale": np.ascontiguousarray(np.asarray(L0_ffn_per_expert_scale,
                                                    np.float32)),
    }
    nc = build_program()
    in_maps = [dict(common, x=np.ascontiguousarray(x[b])) for b in range(NCORES)]
    res = run_bass_kernel_spmd(nc, in_maps, core_ids=list(range(NCORES)))
    return np.stack([np.asarray(res.results[b]["y"], np.float32)
                     for b in range(NCORES)], axis=0)


# version 11
# speedup vs baseline: 1.0411x; 1.0411x over previous
import math
import numpy as np
import ml_dtypes
import concourse.bass as bass
import concourse.mybir as mybir
from concourse.bass_utils import run_bass_kernel_spmd

F32 = mybir.dt.float32
BF16 = mybir.dt.bfloat16
I32 = mybir.dt.int32
AF = mybir.ActivationFunctionType
ALU = mybir.AluOpType
AX = mybir.AxisListType

NCORES = 8
N = 4096
E = 1024
H = 16
D = 64
HID = 2048
NEXP = 8
TS = 512
NST = N // TS
CH = 128
NCH = TS // CH
PG = 256
NPG = TS // PG
EPS1 = 1e-5
EPS2 = 1e-6
EPSL = 1e-5
RSQRT_D = 1.0 / 32.0
MAGIC = 0x5F3759DF
NEWTON_ITERS = 2

DEBUG_TAPS = False


class TK:

    STREAMS = ("sp", "gp", "pe", "act", "dve")

    def __init__(self, nc):
        self.nc = nc
        self.sems = {}
        for s in self.STREAMS:
            self.sems[(s, "cmp")] = nc.alloc_semaphore(f"S_{s}_c")
        self.ch_sems = {}
        self.ch_count = {}
        self.ch_last = {}
        self.ops = []
        self.last_w = {}
        self.readers = {}

    def _record(self, stream, kind, emit, reads, writes, ch=None):
        idx = len(self.ops)
        deps = {}
        for t in reads:
            if t in self.last_w:
                deps[self.last_w[t]] = True
        for t in writes:
            if t in self.last_w:
                deps.setdefault(self.last_w[t], False)
            for r in self.readers.get(t, ()):
                deps.setdefault(r, False)
        o = dict(stream=stream, kind=kind, emit=emit, deps=deps,
                 inc=False, amt=16 if kind == "dma" else 1, ch=ch)
        if kind == "dma":
            if ch not in self.ch_sems:
                self.ch_sems[ch] = self.nc.alloc_semaphore(
                    f"C_{len(self.ch_sems)}")
            if ch in self.ch_last:
                deps.setdefault(self.ch_last[ch], False)
            self.ch_last[ch] = idx
            self.ch_count[ch] = self.ch_count.get(ch, 0) + 1
            o["chidx"] = self.ch_count[ch]
        self.ops.append(o)
        for t in reads:
            self.readers.setdefault(t, []).append(idx)
        for t in writes:
            self.last_w[t] = idx
            self.readers[t] = []
        return idx

    def op(self, stream, method, *args, reads=(), writes=(), **kw):
        def emit(eng, args=args, kw=kw, method=method):
            return getattr(eng, method)(*args, **kw)
        return self._record(stream, "cmp", emit, reads, writes)

    def raw(self, stream, fn, reads=(), writes=()):
        return self._record(stream, "cmp", fn, reads, writes)

    def dma(self, stream, out, in_, reads=(), writes=(), ch=None, **kw):
        def emit(eng, out=out, in_=in_, kw=kw):
            return eng.dma_start(out=out, in_=in_, **kw)
        if ch is None:
            ch = writes[0] if writes else ("misc", stream)
        return self._record(stream, "dma", emit, reads, writes, ch=ch)

    def finalize_and_emit(self, block, engines_map):
        ops = self.ops
        for o in ops:
            best = {}
            for d, is_raw in o["deps"].items():
                p = ops[d]
                if (p["stream"] == o["stream"] and p["kind"] == "cmp"
                        and o["kind"] == "cmp"
                        and (not is_raw or p["stream"] == "pe")):
                    continue
                key = ("ch", p["ch"]) if p["kind"] == "dma" else (p["stream"], "cmp")
                if d > best.get(key, -1):
                    best[key] = d
            o["deps"] = set(best.values())
            for d in o["deps"]:
                ops[d]["inc"] = True
        for o in ops:
            if o["kind"] == "dma":
                o["inc"] = True
        totals = {k: 0 for k in self.sems}
        totals.update({("ch", c): 0 for c in self.ch_sems})
        for o in ops:
            key = ("ch", o["ch"]) if o["kind"] == "dma" else (o["stream"], "cmp")
            if o["inc"]:
                totals[key] += o["amt"]
            o["prefix"] = totals[key]
        self.totals = totals

        def sem_of(key):
            return self.ch_sems[key[1]] if key[0] == "ch" else self.sems[key]

        for s in self.STREAMS:
            s_ops = [o for o in ops if o["stream"] == s]
            if not s_ops:
                continue

            def body(eng, s_ops=s_ops):
                waited = {}
                for o in s_ops:
                    need = {}
                    for d in o["deps"]:
                        p = self.ops[d]
                        key = ("ch", p["ch"]) if p["kind"] == "dma" else (p["stream"], "cmp")
                        v = p["prefix"]
                        if v > waited.get(key, 0):
                            need[key] = max(need.get(key, 0), v)
                    for key, v in need.items():
                        eng.wait_ge(sem_of(key), v)
                        waited[key] = v
                    ins = o["emit"](eng)
                    if o["inc"]:
                        key = (("ch", o["ch"]) if o["kind"] == "dma"
                               else (o["stream"], "cmp"))
                        ins.then_inc(sem_of(key), o["amt"])

            engines_map[s](body)


def _rope_tables():
    i = np.arange(D // 2, dtype=np.float32)
    inv_freq = np.power(np.float32(10000.0),
                        -(i * np.float32(2.0)) / np.float32(D)).astype(np.float32)
    pos = np.arange(N, dtype=np.float32)
    ang = (pos[:, None] * inv_freq[None, :]).astype(np.float32)
    cos = np.cos(ang).astype(np.float32)
    sin = np.sin(ang).astype(np.float32)
    rows = np.arange(128) % 64
    C = cos[:, rows % 32].T.copy()
    Ssig = sin[:, rows % 32].T.copy()
    Ssig[rows >= 32, :] *= -1.0
    return C.astype(np.float32), Ssig.astype(np.float32)


def build_program():
    nc = bass.Bass("TRN2", target_bir_lowering=False, debug=False)

    x_in = nc.dram_tensor("x", [N, E], F32, kind="ExternalInput").ap()
    Wq = nc.dram_tensor("Wq", [E, E], F32, kind="ExternalInput").ap()
    Wk = nc.dram_tensor("Wk", [E, E], F32, kind="ExternalInput").ap()
    Wv = nc.dram_tensor("Wv", [E, E], F32, kind="ExternalInput").ap()
    Wo = nc.dram_tensor("Wo", [E, E], F32, kind="ExternalInput").ap()
    Wg1 = nc.dram_tensor("Wg1", [E, D], F32, kind="ExternalInput").ap()
    Wg2 = nc.dram_tensor("Wg2", [D, E], F32, kind="ExternalInput").ap()
    gamma = nc.dram_tensor("gamma", [E], F32, kind="ExternalInput").ap()
    beta = nc.dram_tensor("beta", [E], F32, kind="ExternalInput").ap()
    r_scale = nc.dram_tensor("r_scale", [E], F32, kind="ExternalInput").ap()
    r_w = nc.dram_tensor("r_w", [E, NEXP], F32, kind="ExternalInput").ap()
    g_w = nc.dram_tensor("g_w", [NEXP, 2, HID, E], F32, kind="ExternalInput").ap()
    l_w = nc.dram_tensor("l_w", [NEXP, HID, E], F32, kind="ExternalInput").ap()
    pe_scale = nc.dram_tensor("pe_scale", [NEXP], F32, kind="ExternalInput").ap()
    y_out = nc.dram_tensor("y", [N, E], F32, kind="ExternalOutput").ap()

    qkv_t = nc.dram_tensor("qkv_t", [24, 128, 8 * 128], BF16, kind="Internal").ap()
    wbf_tmp = nc.dram_tensor("wbf_tmp", [E, E], BF16, kind="Internal").ap()
    gwbf = nc.dram_tensor("gwbf", [NEXP, 2, HID, E], BF16, kind="Internal").ap()
    gwT = nc.dram_tensor("gwT", [NEXP, 2, HID // 128, 128, 8 * 128], BF16,
                         kind="Internal").ap()
    lwbf = nc.dram_tensor("lwbf", [NEXP, HID, E], BF16, kind="Internal").ap()
    wobf = nc.dram_tensor("wobf", [E, E], BF16, kind="Internal").ap()

    C_np, S_np = _rope_tables()
    c_cos = nc.inline_tensor(C_np, name="c_cos").ap()
    c_sin = nc.inline_tensor(S_np, name="c_sin").ap()
    mask_np = (np.arange(128)[:, None] <= np.arange(128)[None, :]).astype(np.float32)
    c_mask = nc.inline_tensor(np.tile(mask_np, (1, 4)).copy(), name="c_mask").ap()
    c_idb = nc.inline_tensor(np.eye(128, dtype=ml_dtypes.bfloat16), name="c_idb").ap()
    c_idf = nc.inline_tensor(np.eye(128, dtype=np.float32), name="c_idf").ap()
    c_ones = nc.inline_tensor(np.ones((1, 128), np.float32), name="c_ones").ap()
    sel_np = np.zeros((8, 8 * 128), np.float32)
    for e in range(8):
        sel_np[e, e * 128:(e + 1) * 128] = 1.0
    c_sel = nc.inline_tensor(sel_np, name="c_sel").ap()
    perm_np = np.zeros((128, 128), np.float32)
    for _i in range(128):
        perm_np[_i ^ 32, _i] = 1.0
    c_perm = nc.inline_tensor(perm_np, name="c_perm").ap()

    A = nc.alloc_sbuf_tensor

    xn_f = A("xn_f", [128, 8 * TS], BF16)
    acc = A("acc", [128, NCH * E], F32)
    Wg1_sb = A("Wg1_sb", [128, 8 * D], BF16)
    Wg2_sb = A("Wg2_sb", [D, E], BF16)
    rwp_sb = A("rwp_sb", [128, 8 * NEXP], F32)
    wqkv_sl = A("wqkv_sl", [128, 4 * 1024], BF16)
    gs0_sl = A("gs0_sl", [128, 4 * 1024], BF16)
    gs1_sl = A("gs1_sl", [128, 4 * 1024], BF16)
    lw_sl = A("lw_sl", [128, 4 * 1024], BF16)
    act_sb = A("act_sb", [128, 16 * TS], BF16)
    Wo_sb = nc.alloc_sbuf_tensor_at(
        "Wo_sb", [128, 8 * E], BF16, offset=nc.lookup_mloc(act_sb).addr)
    q_ro = A("q_ro", [128, 8 * PG], BF16)
    k_ro = A("k_ro", [128, 8 * PG], BF16)
    v_f = A("v_f", [128, 8 * PG], BF16)
    k_tok = A("k_tok", [128, E], BF16)
    v_tok = A("v_tok", [128, E], BF16)
    qs_f = A("qs_f", [128, PG], F32)
    ks_f = A("ks_f", [128, PG], F32)
    rt_f = A("rt_f", [128, PG], F32)
    rt2_f = A("rt2_f", [128, PG], F32)
    ctab = A("ctab", [128, PG], F32)
    stab = A("stab", [128, PG], F32)
    Fm = A("Fm", [128, H * 128], BF16)
    S_f = A("S_f", [128, 512], F32)
    S_b = A("S_b", [128, 512], BF16)
    g1T = A("g1T", [D, TS], BF16)
    gate_t = A("gate_t", [128, E], F32)
    og = A("og", [128, E], F32)
    z_bf = A("z_bf", [128, E], BF16)
    zT = A("zT", [128, E], BF16)
    sq_scr = A("sq_scr", [128, E], F32)
    gel = A("gel", [128, TS], F32)
    a1 = A("a1", [128, TS], F32)
    c_rep = A("c_rep", [128, TS], F32)
    gam_rep = A("gam_rep", [128, E], F32)
    bWo_rep = A("bWo_rep", [128, E], F32)
    pes_rep = A("pes_rep", [128, NEXP], F32)
    mask_sb = A("mask_sb", [128, 512], F32)
    perm_sb = A("perm_sb", [128, 128], F32)
    idb_sb = A("idb_sb", [128, 128], BF16)
    idf_sb = A("idf_sb", [128, 128], F32)
    ones_sb = A("ones_sb", [1, 128], F32)
    sel_sb = A("sel_sb", [8, 8 * 128], F32)
    xin = A("xin", [128, 2 * E], F32)
    out_sb = A("out_sb", [128, 2 * E], F32)
    cT_sb = A("cT_sb", [8, NCH * 128], F32)
    lg = A("lg", [128, NEXP], F32)
    t8a = A("t8a", [128, NEXP], F32)
    t8b = A("t8b", [128, NEXP], F32)
    c_sb = A("c_sb", [128, NEXP], F32)
    col = {k: A(f"col_{k}", [128, 1], F32) for k in
           ("sum", "sum2", "m", "rstd", "rstd1", "v", "t1", "t2", "m2", "r")}
    coli = A("coli", [128, 1], I32)
    rs8 = A("rs8", [8, 128], F32)
    rsT = A("rsT", [128, 8], F32)
    row_f = A("row_f", [1, E], F32)
    ph0 = A("ph0", [128, 16384], BF16)

    PS = nc.alloc_psum_tensor("PS", [128, 4096], F32)

    def bank(b, lo=0, hi=512):
        return PS[:, b * 512 + lo: b * 512 + hi]

    def bank_bf(b, lo=0, n=64):
        return PS[:, b * 512 + lo: b * 512 + lo + n].bitcast(BF16)

    tk = TK(nc)
    SP, GP, PE, ACT, DVE = "sp", "gp", "pe", "act", "dve"

    taps = {}

    def tap(name, ap, reads):
        if not DEBUG_TAPS:
            return
        t = nc.dram_tensor("tap_" + name, list(ap.shape), ap.dtype,
                           kind="ExternalOutput").ap()
        taps[name] = t
        tk.dma(GP, t, ap, reads=reads)

    tk.dma(GP, mask_sb[:, :], c_mask[:, :], writes=[("mask",)], ch="init")
    tk.dma(GP, perm_sb[:, :], c_perm[:, :], writes=[("perm",)], ch="init")
    tk.dma(GP, idb_sb[:, :], c_idb[:, :], writes=[("idb",)], ch="init")
    tk.dma(GP, idf_sb[:, :], c_idf[:, :], writes=[("idf",)], ch="init")
    tk.dma(GP, ones_sb[:, :], c_ones[:, :], writes=[("ones",)], ch="init")
    tk.dma(GP, sel_sb[:, :], c_sel[:, :], writes=[("sel",)], ch="init")

    PH0R = lambda lo, hi: [("ph0", r) for r in range(lo, hi)]
    for m, W in enumerate((Wq, Wk, Wv)):
        tk.dma(GP, wbf_tmp[:, :], W[:, :], writes=[("wbft",)])
        wall = ph0[:, 0:8192].rearrange("p (dc o) -> p dc o", dc=8)
        tk.dma(GP, wall,
               wbf_tmp.rearrange("(dc p) o -> p dc o", p=128),
               reads=[("wbft",)], writes=PH0R(0, 4))
        for oc in range(8):
            tk.dma(SP, qkv_t[m * 8 + oc, :, :].rearrange("p (dc o) -> p dc o", dc=8),
                   wall[:, :, oc * 128:(oc + 1) * 128],
                   reads=PH0R(0, 4), writes=[("qkvt", m, oc)], ch="qkvt")
    tk.dma(GP, wobf[:, :], Wo[:, :], writes=[("wobf",)], ch="init")

    def emit_wo_load():
        for dc in range(8):
            tk.dma(SP, Wo_sb[:, dc * E:(dc + 1) * E], wobf[dc * 128:(dc + 1) * 128, :],
                   reads=[("wobf",)],
                   writes=[("wo",)] + [("actk", hh) for hh in range(HID // 128)], ch="wo")
    emit_wo_load()
    for dc in range(8):
        tk.dma(GP, Wg1_sb[:, dc * D:(dc + 1) * D], Wg1[dc * 128:(dc + 1) * 128, :],
               writes=[("wg1",)], ch="init")
    tk.dma(GP, Wg2_sb[:, :], Wg2[:, :], writes=[("wg2",)], ch="init")

    def emit_conv_casts(e):
        for i in range(2):
            tk.dma(GP, gwbf[e, i, :, :], g_w[e, i, :, :], writes=[("gwbf", e, i)], ch="gwbf")
        tk.dma(GP, lwbf[e, :, :], l_w[e, :, :], writes=[("lwbf", e)], ch="lwbf")

    def emit_conv_expert(e):
        for i in range(2):
            for dc in range(8):
                tk.dma(SP, ph0[:, dc * 2048:(dc + 1) * 2048],
                       gwbf[e, i, :, dc * 128:(dc + 1) * 128],
                       transpose=True, reads=[("gwbf", e, i)],
                       writes=[("ph0", dc)])
            ph0v = ph0[:, :].rearrange("p (dc h) -> p dc h", dc=8)
            for hc in range(HID // 128):
                tk.dma(SP, gwT[e, i, hc, :, :].rearrange("p (dc h) -> p dc h", dc=8),
                       ph0v[:, :, hc * 128:(hc + 1) * 128],
                       reads=PH0R(0, 8), writes=[("gwT", e, i, hc)],
                       ch=("gwTw", hc % 4))

    tk.dma(GP, rs8[:, :], r_scale.rearrange("(dc p) -> dc p", p=128), writes=[("rs8",)], ch="init")
    tk.op(PE, "transpose", bank(1, 0, 8), rs8[:, :], idf_sb[0:8, 0:8],
          reads=[("rs8",), ("idf",)], writes=[("ps", 1)])
    tk.op(DVE, "tensor_copy", rsT[:, :], bank(1, 0, 8),
          reads=[("ps", 1)], writes=[("rsT",)])
    for dc in range(8):
        tk.dma(GP, lg[:, :], r_w[dc * 128:(dc + 1) * 128, :], writes=[("lg",)], ch="init")
        tk.op(DVE, "scalar_tensor_tensor", rwp_sb[:, dc * NEXP:(dc + 1) * NEXP],
              lg[:, :], RSQRT_D,
              rsT[:, dc:dc + 1].broadcast_to((128, NEXP)), ALU.mult, ALU.mult,
              reads=[("lg",), ("rsT",)], writes=[("rwp",)])

    tk.dma(GP, row_f[:, :], gamma.unsqueeze(0), writes=[("rowf",)], ch="init")
    for j in range(2):
        tk.op(PE, "matmul", bank(2 + j), ones_sb[:, :], row_f[:, j * 512:(j + 1) * 512],
              start=True, stop=True, reads=[("rowf",), ("ones",)], writes=[("ps", 2 + j)])
    tk.op(DVE, "tensor_copy", gam_rep[:, :], PS[:, 2 * 512:4 * 512],
          reads=[("ps", 2), ("ps", 3)], writes=[("gam",)])
    tk.dma(GP, row_f[0:1, 0:NEXP], pe_scale.unsqueeze(0), writes=[("rowf",)], ch="init")
    tk.op(PE, "matmul", bank(2, 0, NEXP), ones_sb[:, :], row_f[0:1, 0:NEXP],
          start=True, stop=True, reads=[("rowf",), ("ones",)], writes=[("ps", 2)])
    tk.op(DVE, "tensor_copy", pes_rep[:, :], bank(2, 0, NEXP),
          reads=[("ps", 2)], writes=[("pes",)])
    tk.dma(GP, rs8[:, :], beta.rearrange("(dc p) -> dc p", p=128), writes=[("rs8",)], ch="init")
    tk.op(PE, "transpose", bank(1, 0, 8), rs8[:, :], idf_sb[0:8, 0:8],
          reads=[("rs8",), ("idf",)], writes=[("ps", 1)])
    tk.op(DVE, "tensor_copy", rsT[:, :], bank(1, 0, 8),
          reads=[("ps", 1)], writes=[("rsT",)])
    tk.op(DVE, "tensor_copy", z_bf[:, 0:8], rsT[:, :], reads=[("rsT",)],
          writes=[("znb",)])
    for j in range(2):
        for dc in range(8):
            tk.op(PE, "matmul", PS[0:1, (4 + j) * 512:(5 + j) * 512],
                  z_bf[:, dc:dc + 1],
                  Wo_sb[:, dc * E + j * 512: dc * E + (j + 1) * 512],
                  start=(dc == 0), stop=(dc == 7),
                  reads=[("znb",), ("wo",)], writes=[("ps", 4 + j)])
    tk.op(DVE, "tensor_copy", row_f[:, :], PS[0:1, 4 * 512:6 * 512],
          reads=[("ps", 4), ("ps", 5)], writes=[("rowf",)])
    for j in range(2):
        tk.op(PE, "matmul", bank(2 + j), ones_sb[:, :], row_f[:, j * 512:(j + 1) * 512],
              start=True, stop=True, reads=[("rowf",), ("ones",)], writes=[("ps", 2 + j)])
    tk.op(DVE, "tensor_copy", bWo_rep[:, :], PS[:, 2 * 512:4 * 512],
          reads=[("ps", 2), ("ps", 3)], writes=[("bwo",)])

    def newton_rsqrt(vcol, outcol, tags_r, tag_w):
        tk.op(DVE, "tensor_scalar", coli[:, :], vcol.bitcast(I32), 1, None,
              ALU.arith_shift_right, reads=tags_r, writes=[("coli",)])
        tk.op(DVE, "tensor_scalar", coli[:, :], coli[:, :], -1, MAGIC,
              ALU.mult, ALU.add, reads=[("coli",)], writes=[("coli",)])
        for it in range(NEWTON_ITERS):
            tk.op(DVE, "tensor_mul", col["t1"][:, :], coli[:, :].bitcast(F32),
                  coli[:, :].bitcast(F32), reads=[("coli",)], writes=[("c_t1",)])
            tk.op(DVE, "tensor_mul", col["t1"][:, :], col["t1"][:, :], vcol,
                  reads=[("c_t1",)] + list(tags_r), writes=[("c_t1",)])
            tk.op(DVE, "tensor_scalar", col["t1"][:, :], col["t1"][:, :], -0.5, 1.5,
                  ALU.mult, ALU.add, reads=[("c_t1",)], writes=[("c_t1",)])
            last = (it == NEWTON_ITERS - 1)
            tk.op(DVE, "tensor_tensor",
                  outcol if last else coli[:, :].bitcast(F32),
                  coli[:, :].bitcast(F32), col["t1"][:, :], ALU.mult,
                  reads=[("coli",), ("c_t1",)],
                  writes=[tag_w if last else ("coli",)])

    tap("rwp", rwp_sb[:, :], [("rwp",)])
    tap("rowf", row_f[:, :], [("rowf",)])
    tap("rsT2", rsT[:, :], [("rsT",)])
    tap("zb8", z_bf[:, 0:8], [("znb",)])
    tap("gam", gam_rep[:, :], [("gam",)])
    tap("bwo", bWo_rep[:, :], [("bwo",)])
    tap("pes", pes_rep[:, :], [("pes",)])
    tap("wo0", Wo_sb[:, 0:E], [("wo",)])
    tap("qkvt0", qkv_t[0, :, :], [("qkvt", 0, 0)])
    tap("qkvt8", qkv_t[8, :, :], [("qkvt", 1, 0)])
    tap("gwt0", gwT[0, 0, 0, :, :], [("gwT", 0, 0, 0)])
    tap("lwbf0", lwbf[0, 0:128, :], [("lwbf", 0)])

    tk.op(DVE, "memset", S_f[:, :], 0.0, writes=[("sf",)])
    tk.op(DVE, "memset", S_b[:, :], 0.0, writes=[("sb",)])

    def emit_passA(st):
        t0 = st * TS

        for i in range(NCH):
            par = i % 2
            xv = xin[:, par * E:(par + 1) * E]
            tk.dma(ACT, xv, x_in[t0 + i * CH: t0 + (i + 1) * CH, :],
                   writes=[("xin", par)])
            tk.op(ACT, "activation", og[:, :], xv, AF.Copy,
                  accum_out=col["sum"][:, :],
                  reads=[("xin", par)], writes=[("og",), ("csum",)])
            tk.op(ACT, "activation", sq_scr[:, :], xv, AF.Square,
                  accum_out=col["sum2"][:, :],
                  reads=[("xin", par)], writes=[("sqscr",), ("csum2",)])
            tk.op(DVE, "tensor_scalar_mul", col["m"][:, :], col["sum"][:, :],
                  1.0 / E, reads=[("csum",)], writes=[("cm",)])
            tk.op(DVE, "tensor_mul", col["t2"][:, :], col["m"][:, :], col["m"][:, :],
                  reads=[("cm",)], writes=[("ct2",)])
            tk.op(DVE, "scalar_tensor_tensor", col["v"][:, :], col["sum2"][:, :],
                  1.0 / E, col["t2"][:, :], ALU.mult, ALU.subtract,
                  reads=[("csum2",), ("ct2",)], writes=[("cv",)])
            tk.op(DVE, "tensor_scalar_add", col["v"][:, :], col["v"][:, :], EPS1,
                  reads=[("cv",)], writes=[("cv",)])
            newton_rsqrt(col["v"][:, :], col["rstd"][:, :], [("cv",)], ("crstd", 0))
            tk.op(DVE, "tensor_scalar", sq_scr[:, :], xv, col["m"][:, :],
                  col["rstd"][:, :], ALU.subtract, ALU.mult,
                  reads=[("xin", par), ("cm",), ("crstd", 0)], writes=[("sqscr",)])
            tk.op(ACT, "activation", z_bf[:, :], sq_scr[:, :], AF.Copy,
                  reads=[("sqscr",)], writes=[("znb",)])
            for dc in range(8):
                tk.op(PE, "transpose", bank_bf(7, dc * 64, 64),
                      z_bf[:, dc * 128:(dc + 1) * 128], idb_sb[:, :],
                      reads=[("znb",), ("idb",)], writes=[("ps", 7)])
            tk.op(DVE, "tensor_copy",
                  xn_f[:, :].rearrange("p (dc t) -> p dc t", dc=8)[
                      :, :, i * CH:(i + 1) * CH],
                  bank_bf(7, 0, 512).rearrange("p (a b) -> p a b", a=8),
                  reads=[("ps", 7)],
                  writes=[("xnf", dc, i) for dc in range(8)])
            tk.op(DVE, "reciprocal", col["t1"][:, :], col["v"][:, :],
                  reads=[("cv",)], writes=[("c_t1",)])
            tk.op(DVE, "tensor_scalar", col["t2"][:, :], col["t1"][:, :], -EPS1,
                  1.0, ALU.mult, ALU.add, reads=[("c_t1",)], writes=[("ct2",)])
            tk.op(DVE, "tensor_scalar_add", col["t2"][:, :], col["t2"][:, :], EPS2,
                  reads=[("ct2",)], writes=[("ct2",)])
            newton_rsqrt(col["t2"][:, :], col["r"][:, :], [("ct2",)], ("c_r",))
            for half in range(2):
                for s4 in range(4):
                    dc = half * 4 + s4
                    tk.op(PE, "transpose",
                          PS[:, 7 * 512 + s4 * 128: 7 * 512 + (s4 + 1) * 128],
                          sq_scr[:, dc * 128:(dc + 1) * 128], idf_sb[:, :],
                          reads=[("sqscr",), ("idf",)], writes=[("ps", 7)])
                tk.op(DVE, "tensor_copy",
                      og[:, half * 512:(half + 1) * 512],
                      PS[:, 7 * 512:7 * 512 + 512],
                      reads=[("ps", 7)], writes=[("og",)])
            for dc in range(8):
                tk.op(PE, "matmul", bank(0, 0, NEXP),
                      og[:, dc * 128:(dc + 1) * 128],
                      rwp_sb[:, dc * NEXP:(dc + 1) * NEXP],
                      start=(dc == 0), stop=(dc == 7),
                      reads=[("og",), ("rwp",)], writes=[("ps", 0)])
            tk.op(DVE, "tensor_scalar_mul", lg[:, :], bank(0, 0, NEXP),
                  col["r"][:, :], reads=[("ps", 0), ("c_r",)], writes=[("lg",)])
            tk.op(DVE, "reduce_max", col["m"][:, :], lg[:, :], axis=AX.X,
                  reads=[("lg",)], writes=[("cm",)])
            tk.op(DVE, "tensor_scalar", t8a[:, :], lg[:, :], col["m"][:, :], None,
                  ALU.is_ge, reads=[("lg",), ("cm",)], writes=[("t8a",)])
            tk.op(DVE, "scalar_tensor_tensor", t8b[:, :], t8a[:, :], -1e30,
                  lg[:, :], ALU.mult, ALU.add, reads=[("t8a",), ("lg",)],
                  writes=[("t8b",)])
            tk.op(DVE, "reduce_max", col["m2"][:, :], t8b[:, :], axis=AX.X,
                  reads=[("t8b",)], writes=[("cm2",)])
            tk.op(DVE, "tensor_scalar_mul", col["t1"][:, :], col["m"][:, :], -1.0,
                  reads=[("cm",)], writes=[("c_t1",)])
            tk.op(ACT, "activation", col["t2"][:, :], col["m2"][:, :], AF.Exp,
                  bias=col["t1"][:, :], scale=1.0,
                  reads=[("cm2",), ("c_t1",)], writes=[("ct2",)])
            tk.op(DVE, "tensor_scalar_add", col["t2"][:, :], col["t2"][:, :], 1.0,
                  reads=[("ct2",)], writes=[("ct2",)])
            tk.op(DVE, "reciprocal", col["r"][:, :], col["t2"][:, :],
                  reads=[("ct2",)], writes=[("c_r",)])
            tk.op(ACT, "activation", t8a[:, :], lg[:, :], AF.Exp,
                  bias=col["t1"][:, :], scale=1.0,
                  reads=[("lg",), ("c_t1",)], writes=[("t8a",)])
            tk.op(DVE, "tensor_scalar", t8b[:, :], lg[:, :], col["m2"][:, :], None,
                  ALU.is_ge, reads=[("lg",), ("cm2",)], writes=[("t8b",)])
            tk.op(DVE, "tensor_mul", t8a[:, :], t8a[:, :], t8b[:, :],
                  reads=[("t8a",), ("t8b",)], writes=[("t8a",)])
            tk.op(DVE, "tensor_scalar_mul", t8a[:, :], t8a[:, :], col["r"][:, :],
                  reads=[("t8a",), ("c_r",)], writes=[("t8a",)])
            tk.op(DVE, "tensor_mul", c_sb[:, :], t8a[:, :], pes_rep[:, :],
                  reads=[("t8a",), ("pes",)], writes=[("csb",)])
            tk.op(PE, "transpose", PS[0:NEXP, 1 * 512:1 * 512 + 128], c_sb[:, :],
                  idf_sb[:, :], reads=[("csb",), ("idf",)], writes=[("ps", 1)])
            tk.op(DVE, "tensor_copy", cT_sb[:, i * 128:(i + 1) * 128],
                  PS[0:NEXP, 1 * 512:1 * 512 + 128],
                  reads=[("ps", 1)], writes=[("cT", i)])

        if st == 0:
            tap("xnf", xn_f[:, :],
                [("xnf", dc, ii) for dc in range(8) for ii in range(NCH)])
            tap("ct", cT_sb[:, :], [("cT", ii) for ii in range(NCH)])

    def emit_attn(st):
        t0 = st * TS
        pend = [None]
        if st >= 1:
            emit_wo_load()
        for dc in range(8):
            tk.op(PE, "matmul", PS[0:D, 0:TS], Wg1_sb[:, dc * D:(dc + 1) * D],
                  xn_f[:, dc * TS:(dc + 1) * TS], start=(dc == 0), stop=(dc == 7),
                  reads=[("wg1",)] + [("xnf", dc, ii) for ii in range(NCH)],
                  writes=[("ps", 0)])
        tk.op(DVE, "tensor_copy", g1T[:, :], PS[0:D, 0:TS],
              reads=[("ps", 0)], writes=[("g1T",)])

        for g in range(NPG):
            gt0 = t0 + g * PG
            tk.dma(ACT, ctab[:, :], c_cos[:, gt0:gt0 + PG], writes=[("ctab",)])
            tk.dma(ACT, stab[:, :], c_sin[:, gt0:gt0 + PG], writes=[("stab",)])
            for oc in range(24):
                par = oc % 4
                sl = wqkv_sl[:, par * 1024:(par + 1) * 1024]
                tk.dma(SP, sl, qkv_t[oc, :, :], writes=[("wsl", par)])
                for dc in range(8):
                    tk.op(PE, "matmul", bank(par, 0, PG),
                          sl[:, dc * 128:(dc + 1) * 128],
                          xn_f[:, dc * TS + g * PG: dc * TS + (g + 1) * PG],
                          start=(dc == 0), stop=(dc == 7),
                          reads=[("wsl", par)] +
                                [("xnf", dc, g * 2), ("xnf", dc, g * 2 + 1)],
                          writes=[("ps", par)])
                m, fc = divmod(oc, 8)
                if m < 2:
                    s_out = qs_f if m == 0 else ks_f
                    s_tag = ("qs",) if m == 0 else ("ks",)
                    dst = q_ro if m == 0 else k_ro
                    d_tag = ("qro", fc) if m == 0 else ("kro", fc)
                    swb = 4 + oc % 2
                    tk.op(ACT, "activation", s_out[:, :], bank(par, 0, PG),
                          AF.Silu, reads=[("ps", par)], writes=[s_tag])
                    tk.op(PE, "matmul", bank(swb, 0, PG), perm_sb[:, :],
                          s_out[:, :], start=True, stop=True,
                          reads=[s_tag, ("perm",)], writes=[("ps", swb)])
                    tk.op(DVE, "tensor_mul", rt_f[:, :], s_out[:, :], ctab[:, :],
                          reads=[s_tag, ("ctab",)], writes=[("rt",)])
                    tk.op(DVE, "tensor_mul", rt2_f[:, :], bank(swb, 0, PG),
                          stab[:, :], reads=[("ps", swb), ("stab",)],
                          writes=[("rt2",)])
                    tk.op(DVE, "tensor_sub", dst[:, fc * PG:(fc + 1) * PG],
                          rt_f[:, :], rt2_f[:, :],
                          reads=[("rt",), ("rt2",)], writes=[d_tag])
                else:
                    tk.op(ACT, "activation", v_f[:, fc * PG:(fc + 1) * PG],
                          bank(par, 0, PG), AF.Copy,
                          reads=[("ps", par)], writes=[("vf", fc)])

            if st == 0 and g == 0:
                tap("qro", q_ro[:, :], [("qro", fc) for fc in range(8)])
                tap("kro", k_ro[:, :], [("kro", fc) for fc in range(8)])
                tap("vfm", v_f[:, :], [("vf", fc) for fc in range(8)])
                tap("g1t", g1T[:, :], [("g1T",)])

            for cc in range(2):
                i = g * 2 + cc
                for fc in range(8):
                    tk.op(PE, "transpose", bank_bf(7, fc * 64, 64),
                          k_ro[:, fc * PG + cc * 128: fc * PG + (cc + 1) * 128],
                          idb_sb[:, :],
                          reads=[("kro", fc), ("idb",)], writes=[("ps", 7)])
                tk.op(DVE, "tensor_copy", k_tok[:, :], bank_bf(7, 0, 512),
                      reads=[("ps", 7)],
                      writes=[("ktok", fc) for fc in range(8)])
                for fc in range(8):
                    tk.op(PE, "transpose", bank_bf(7, fc * 64, 64),
                          v_f[:, fc * PG + cc * 128: fc * PG + (cc + 1) * 128],
                          idb_sb[:, :],
                          reads=[("vf", fc), ("idb",)], writes=[("ps", 7)])
                tk.op(DVE, "tensor_copy", v_tok[:, :], bank_bf(7, 0, 512),
                      reads=[("ps", 7)],
                      writes=[("vtok", fc) for fc in range(8)])
                for h in range(H):
                    fc, half = divmod(h, 2)
                    rows = slice(half * 64, half * 64 + 64)
                    fb = 2 + h % 2
                    tk.op(PE, "matmul", bank(fb, 0, 128),
                          k_ro[rows, fc * PG + cc * 128: fc * PG + (cc + 1) * 128],
                          q_ro[rows, fc * PG + cc * 128: fc * PG + (cc + 1) * 128],
                          start=True, stop=True,
                          reads=[("kro", fc), ("qro", fc)], writes=[("ps", fb)])
                    tk.op(DVE, "tensor_mul", Fm[:, h * 128:(h + 1) * 128],
                          bank(fb, 0, 128), mask_sb[:, 0:128],
                          reads=[("ps", fb), ("mask",)], writes=[("fm", h // 4)])
                for j in range(2):
                    tk.op(PE, "matmul", bank(j, 0, 512),
                          g1T[:, i * 128:(i + 1) * 128],
                          Wg2_sb[:, j * 512:(j + 1) * 512],
                          start=True, stop=True,
                          reads=[("g1T",), ("wg2",)], writes=[("ps", j)])
                tk.op(ACT, "activation", gate_t[:, :], PS[:, 0:1024], AF.Sigmoid,
                      reads=[("ps", 0), ("ps", 1)], writes=[("gatet",)])
                for h in range(H):
                    fc, half = divmod(h, 2)
                    rows = slice(half * 64, half * 64 + 64)
                    ob, olo = divmod(h * 64, 512)
                    tk.op(PE, "matmul", bank(4 + ob, olo, olo + 64),
                          Fm[:, h * 128:(h + 1) * 128], v_tok[:, h * 64:(h + 1) * 64],
                          start=True, stop=False,
                          reads=[("fm", h // 4), ("vtok", h // 2)],
                          writes=[("ps", 4 + ob)])
                    tk.op(PE, "matmul", bank(4 + ob, olo, olo + 64),
                          q_ro[rows, fc * PG + cc * 128: fc * PG + (cc + 1) * 128],
                          S_b[rows, (h // 2) * 64:(h // 2) * 64 + 64],
                          start=False, stop=True,
                          reads=[("qro", fc), ("sb",)], writes=[("ps", 4 + ob)])
                for h in range(H):
                    half = h % 2
                    tk.op(PE, "matmul",
                          PS[half * 64:half * 64 + 64,
                             6 * 512 + (h // 2) * 64: 6 * 512 + (h // 2) * 64 + 64],
                          k_tok[:, h * 64:(h + 1) * 64], v_tok[:, h * 64:(h + 1) * 64],
                          start=(h < 2), stop=(h >= H - 2),
                          reads=[("ktok", h // 2), ("vtok", h // 2)],
                          writes=[("ps", 6)])
                tk.op(DVE, "tensor_add", S_f[:, :], S_f[:, :], bank(6),
                      reads=[("ps", 6), ("sf",)], writes=[("sf",)])
                tk.op(DVE, "tensor_copy", S_b[:, :], S_f[:, :],
                      reads=[("sf",)], writes=[("sb",)])
                tk.op(DVE, "tensor_mul", og[:, :], PS[:, 4 * 512:6 * 512],
                      gate_t[:, :],
                      reads=[("ps", 4), ("ps", 5), ("gatet",)], writes=[("og",)])
                tk.op(ACT, "activation", gate_t[:, :], og[:, :], AF.Copy,
                      accum_out=col["sum"][:, :],
                      reads=[("og",), ("gatet",)], writes=[("gatet",), ("csum",)])
                tk.op(ACT, "activation", sq_scr[:, :], og[:, :], AF.Square,
                      accum_out=col["sum2"][:, :],
                      reads=[("og",)], writes=[("sqscr",), ("csum2",)])
                rcol = col["rstd"] if i % 2 == 0 else col["rstd1"]
                rtag = ("crstd", i % 2)

                def s1b(i=i, rcol=rcol, rtag=rtag):
                    tk.op(DVE, "tensor_scalar_mul", col["m"][:, :], col["sum"][:, :],
                          1.0 / E, reads=[("csum",)], writes=[("cm",)])
                    tk.op(DVE, "tensor_mul", col["t2"][:, :], col["m"][:, :],
                          col["m"][:, :], reads=[("cm",)], writes=[("ct2",)])
                    tk.op(DVE, "scalar_tensor_tensor", col["v"][:, :], col["sum2"][:, :],
                          1.0 / E, col["t2"][:, :], ALU.mult, ALU.subtract,
                          reads=[("csum2",), ("ct2",)], writes=[("cv",)])
                    tk.op(DVE, "tensor_scalar_add", col["v"][:, :], col["v"][:, :], EPSL,
                          reads=[("cv",)], writes=[("cv",)])
                    newton_rsqrt(col["v"][:, :], rcol[:, :], [("cv",)], rtag)
                    tk.op(DVE, "scalar_tensor_tensor", z_bf[:, :], og[:, :],
                          col["m"][:, :], gam_rep[:, :], ALU.subtract, ALU.mult,
                          reads=[("og",), ("cm",), ("gam",)], writes=[("znb",)])

                if pend[0] is not None:
                    pend[0]()
                s1b()
                if st == 0 and i == 0:
                    tap("ktok0", k_tok[:, :], [("ktok", fc) for fc in range(8)])
                    tap("vtok0", v_tok[:, :], [("vtok", fc) for fc in range(8)])
                    tap("fm0", Fm[:, :], [("fm", h) for h in range(H)])
                    tap("sf0", S_f[:, :], [("sf",)])
                    tap("gatet0", gate_t[:, :], [("gatet",)])
                    tap("og0", og[:, :], [("og",)])
                    tap("zbf0", z_bf[:, :], [("znb",)])

                def epiB(i=i, rcol=rcol, rtag=rtag):
                    for dc in range(8):
                        tk.op(PE, "transpose", bank_bf(7, dc * 64, 64),
                              z_bf[:, dc * 128:(dc + 1) * 128], idb_sb[:, :],
                              reads=[("znb",), ("idb",)], writes=[("ps", 7)])
                    tk.op(DVE, "tensor_copy", zT[:, :], bank_bf(7, 0, 512),
                          reads=[("ps", 7)],
                          writes=[("zT", dc) for dc in range(8)])
                    for jh in range(2):
                        for dc in range(8):
                            tk.op(PE, "matmul", bank(jh, 0, 512),
                                  zT[:, dc * 128:(dc + 1) * 128],
                                  Wo_sb[:, dc * E + jh * 512: dc * E + (jh + 1) * 512],
                                  start=(dc == 0), stop=(dc == 7),
                                  reads=[("zT", dc), ("wo",)], writes=[("ps", jh)])
                    tk.op(DVE, "scalar_tensor_tensor", acc[:, i * E:(i + 1) * E],
                          PS[:, 0:1024], rcol[:, :], bWo_rep[:, :],
                          ALU.mult, ALU.add,
                          reads=[("ps", 0), ("ps", 1), rtag, ("bwo",)],
                          writes=[("acc", i)])
                    if st == 0 and i == 0:
                        tap("acc0", acc[:, 0:E], [("acc", 0)])

                pend[0] = epiB
        pend[0]()
        pend[0] = None
        tap(f"sfst{st}", S_f[:, :], [("sf",)])
        tap(f"xnfst{st}", xn_f[:, 0:512], [("xnf", 0, ii) for ii in range(NCH)])

    def emit_ffn_expert(st, e):
        if True:
            tk.op(PE, "matmul", bank(6, 0, 512), sel_sb[:, e * 128:(e + 1) * 128],
                  cT_sb[:, :], start=True, stop=True,
                  reads=[("sel",)] + [("cT", ii) for ii in range(NCH)],
                  writes=[("ps", 6)])
            tk.op(DVE, "tensor_copy", c_rep[:, :], bank(6, 0, 512),
                  reads=[("ps", 6)], writes=[("crep",)])
            for hc in range(HID // 128):
                par = hc % 4
                g0b, g1b = (hc % 2) * 2, (hc % 2) * 2 + 1
                s0 = gs0_sl[:, par * 1024:(par + 1) * 1024]
                s1 = gs1_sl[:, par * 1024:(par + 1) * 1024]
                tk.dma(SP, s0, gwT[e, 0, hc, :, :], writes=[("gs0", par)])
                tk.dma(SP, s1, gwT[e, 1, hc, :, :], writes=[("gs1", par)])
                for dc in range(8):
                    tk.op(PE, "matmul", bank(g0b), s0[:, dc * 128:(dc + 1) * 128],
                          xn_f[:, dc * TS:(dc + 1) * TS],
                          start=(dc == 0), stop=(dc == 7),
                          reads=[("gs0", par)] + [("xnf", dc, ii) for ii in range(NCH)],
                          writes=[("ps", g0b)])
                for dc in range(8):
                    tk.op(PE, "matmul", bank(g1b), s1[:, dc * 128:(dc + 1) * 128],
                          xn_f[:, dc * TS:(dc + 1) * TS],
                          start=(dc == 0), stop=(dc == 7),
                          reads=[("gs1", par)] + [("xnf", dc, ii) for ii in range(NCH)],
                          writes=[("ps", g1b)])
                tk.op(ACT, "activation", gel[:, :], bank(g0b), AF.Gelu_apprx_tanh,
                      reads=[("ps", g0b)], writes=[("gel",)])
                tk.op(DVE, "tensor_mul", a1[:, :], gel[:, :], bank(g1b),
                      reads=[("gel",), ("ps", g1b)], writes=[("a1",)])
                tk.op(GP, "tensor_tensor", act_sb[:, hc * TS:(hc + 1) * TS],
                      a1[:, :], c_rep[:, :], ALU.mult,
                      reads=[("a1",), ("crep",)], writes=[("actk", hc), ("wo",)])
            for hc in range(HID // 128):
                par = hc % 4
                sl = lw_sl[:, par * 1024:(par + 1) * 1024]
                tk.dma(SP, sl, lwbf[e, hc * 128:(hc + 1) * 128, :],
                       writes=[("lwsl", par)])
                for isl in range(4):
                    for jh in range(2):
                        b = isl * 2 + jh
                        tk.op(PE, "matmul", bank(b),
                              act_sb[:, hc * TS + isl * 128: hc * TS + (isl + 1) * 128],
                              sl[:, jh * 512:(jh + 1) * 512],
                              start=(hc == 0), stop=(hc == HID // 128 - 1),
                              reads=[("actk", hc), ("lwsl", par)],
                              writes=[("ps", b)])
            for isl in range(4):
                tk.op(DVE, "tensor_add", acc[:, isl * E:(isl + 1) * E],
                      acc[:, isl * E:(isl + 1) * E], PS[:, isl * 1024:(isl + 1) * 1024],
                      reads=[("ps", isl * 2), ("ps", isl * 2 + 1), ("acc", isl)],
                      writes=[("acc", isl)])
            if st == 0 and e == 0:
                tap("actsb0", act_sb[:, :],
                    [("actk", hh) for hh in range(HID // 128)])
                tap("crep0", c_rep[:, :], [("crep",)])
                tap("accf0", acc[:, 0:E], [("acc", 0)])
            if st == 0 and e == 7:
                tap("actsb7", act_sb[:, :],
                    [("actk", hh) for hh in range(HID // 128)])
                tap("crep7", c_rep[:, :], [("crep",)])
                tap("accful", acc[:, :], [("acc", ii) for ii in range(NCH)])
            if e == 7:
                tap(f"accst{st}", acc[:, :], [("acc", ii) for ii in range(NCH)])

    def emit_store(st):
        t0 = st * TS
        for i in range(NCH):
            par = i % 2
            ov = out_sb[:, par * E:(par + 1) * E]
            tk.dma(ACT, ov, x_in[t0 + i * CH: t0 + (i + 1) * CH, :],
                   writes=[("outs", par)])
            tk.op(DVE, "tensor_add", ov, acc[:, i * E:(i + 1) * E], ov,
                  reads=[("outs", par), ("acc", i)], writes=[("outs", par)])
            tk.dma(ACT, y_out[t0 + i * CH: t0 + (i + 1) * CH, :], ov,
                   reads=[("outs", par)], writes=[("y", st, i)], ch="y")

    emit_passA(0)
    for e in range(NEXP):
        emit_conv_casts(e)
    emit_attn(0)
    emit_conv_expert(0)
    emit_conv_expert(1)
    for e in range(NEXP):
        if e + 2 < NEXP:
            emit_conv_expert(e + 2)
        emit_ffn_expert(0, e)
    for st in range(1, NST):
        emit_passA(st)
        emit_store(st - 1)
        emit_attn(st)
        for e in range(NEXP):
            emit_ffn_expert(st, e)
    emit_store(NST - 1)

    with nc.Block() as block:
        engines_map = dict(sp=block.sync, gp=block.gpsimd, pe=block.tensor,
                           act=block.scalar, dve=block.vector)
        tk.finalize_and_emit(block, engines_map)
        def final_sync(eng):
            for key, tot in tk.totals.items():
                if tot > 0:
                    sem = tk.ch_sems[key[1]] if key[0] == "ch" else tk.sems[key]
                    eng.wait_ge(sem, tot)
        block.sync(final_sync)

    return nc


_S_INIT_FIXED = False


def kernel(L0_attn_x, L0_attn_Wq, L0_attn_Wk, L0_attn_Wv, L0_attn_Wo,
           L0_attn_Wg1, L0_attn_Wg2, L0_attn_gamma, L0_attn_beta,
           L0_ffn_router_scale, L0_ffn_router_w, L0_ffn_gating_w,
           L0_ffn_linear_w, L0_ffn_per_expert_scale):
    x = np.ascontiguousarray(np.asarray(L0_attn_x, dtype=np.float32))
    common = {
        "Wq": np.ascontiguousarray(np.asarray(L0_attn_Wq, np.float32)),
        "Wk": np.ascontiguousarray(np.asarray(L0_attn_Wk, np.float32)),
        "Wv": np.ascontiguousarray(np.asarray(L0_attn_Wv, np.float32)),
        "Wo": np.ascontiguousarray(np.asarray(L0_attn_Wo, np.float32)),
        "Wg1": np.ascontiguousarray(np.asarray(L0_attn_Wg1, np.float32)),
        "Wg2": np.ascontiguousarray(np.asarray(L0_attn_Wg2, np.float32)),
        "gamma": np.ascontiguousarray(np.asarray(L0_attn_gamma, np.float32)),
        "beta": np.ascontiguousarray(np.asarray(L0_attn_beta, np.float32)),
        "r_scale": np.ascontiguousarray(np.asarray(L0_ffn_router_scale, np.float32)),
        "r_w": np.ascontiguousarray(np.asarray(L0_ffn_router_w, np.float32)),
        "g_w": np.ascontiguousarray(np.asarray(L0_ffn_gating_w, np.float32)),
        "l_w": np.ascontiguousarray(np.asarray(L0_ffn_linear_w, np.float32)),
        "pe_scale": np.ascontiguousarray(np.asarray(L0_ffn_per_expert_scale,
                                                    np.float32)),
    }
    nc = build_program()
    in_maps = [dict(common, x=np.ascontiguousarray(x[b])) for b in range(NCORES)]
    res = run_bass_kernel_spmd(nc, in_maps, core_ids=list(range(NCORES)))
    return np.stack([np.asarray(res.results[b]["y"], np.float32)
                     for b in range(NCORES)], axis=0)


# version 12
# speedup vs baseline: 1.0856x; 1.0427x over previous
import math
import numpy as np
import ml_dtypes
import concourse.bass as bass
import concourse.mybir as mybir
from concourse.bass_utils import run_bass_kernel_spmd

F32 = mybir.dt.float32
BF16 = mybir.dt.bfloat16
I32 = mybir.dt.int32
AF = mybir.ActivationFunctionType
ALU = mybir.AluOpType
AX = mybir.AxisListType

NCORES = 8
N = 4096
E = 1024
H = 16
D = 64
HID = 2048
NEXP = 8
TS = 512
NST = N // TS
CH = 128
NCH = TS // CH
PG = 256
NPG = TS // PG
EPS1 = 1e-5
EPS2 = 1e-6
EPSL = 1e-5
RSQRT_D = 1.0 / 32.0
MAGIC = 0x5F3759DF
NEWTON_ITERS = 2

DEBUG_TAPS = False


class TK:

    STREAMS = ("sp", "gp", "pe", "act", "dve")

    def __init__(self, nc):
        self.nc = nc
        self.sems = {}
        for s in self.STREAMS:
            self.sems[(s, "cmp")] = nc.alloc_semaphore(f"S_{s}_c")
        self.ch_sems = {}
        self.ch_count = {}
        self.ch_last = {}
        self.ops = []
        self.last_w = {}
        self.readers = {}

    def _record(self, stream, kind, emit, reads, writes, ch=None):
        idx = len(self.ops)
        deps = {}
        for t in reads:
            if t in self.last_w:
                deps[self.last_w[t]] = True
        for t in writes:
            if t in self.last_w:
                deps.setdefault(self.last_w[t], False)
            for r in self.readers.get(t, ()):
                deps.setdefault(r, False)
        o = dict(stream=stream, kind=kind, emit=emit, deps=deps,
                 inc=False, amt=16 if kind == "dma" else 1, ch=ch)
        if kind == "dma":
            if ch not in self.ch_sems:
                self.ch_sems[ch] = self.nc.alloc_semaphore(
                    f"C_{len(self.ch_sems)}")
            if ch in self.ch_last:
                deps.setdefault(self.ch_last[ch], False)
            self.ch_last[ch] = idx
            self.ch_count[ch] = self.ch_count.get(ch, 0) + 1
            o["chidx"] = self.ch_count[ch]
        self.ops.append(o)
        for t in reads:
            self.readers.setdefault(t, []).append(idx)
        for t in writes:
            self.last_w[t] = idx
            self.readers[t] = []
        return idx

    def op(self, stream, method, *args, reads=(), writes=(), **kw):
        def emit(eng, args=args, kw=kw, method=method):
            return getattr(eng, method)(*args, **kw)
        return self._record(stream, "cmp", emit, reads, writes)

    def raw(self, stream, fn, reads=(), writes=()):
        return self._record(stream, "cmp", fn, reads, writes)

    def dma(self, stream, out, in_, reads=(), writes=(), ch=None, **kw):
        def emit(eng, out=out, in_=in_, kw=kw):
            return eng.dma_start(out=out, in_=in_, **kw)
        if ch is None:
            ch = writes[0] if writes else ("misc", stream)
        return self._record(stream, "dma", emit, reads, writes, ch=ch)

    def finalize_and_emit(self, block, engines_map):
        ops = self.ops
        for o in ops:
            best = {}
            for d, is_raw in o["deps"].items():
                p = ops[d]
                if (p["stream"] == o["stream"] and p["kind"] == "cmp"
                        and o["kind"] == "cmp"
                        and (not is_raw or p["stream"] == "pe")):
                    continue
                key = ("ch", p["ch"]) if p["kind"] == "dma" else (p["stream"], "cmp")
                if d > best.get(key, -1):
                    best[key] = d
            o["deps"] = set(best.values())
            for d in o["deps"]:
                ops[d]["inc"] = True
        for o in ops:
            if o["kind"] == "dma":
                o["inc"] = True
        totals = {k: 0 for k in self.sems}
        totals.update({("ch", c): 0 for c in self.ch_sems})
        for o in ops:
            key = ("ch", o["ch"]) if o["kind"] == "dma" else (o["stream"], "cmp")
            if o["inc"]:
                totals[key] += o["amt"]
            o["prefix"] = totals[key]
        self.totals = totals

        def sem_of(key):
            return self.ch_sems[key[1]] if key[0] == "ch" else self.sems[key]

        for s in self.STREAMS:
            s_ops = [o for o in ops if o["stream"] == s]
            if not s_ops:
                continue

            def body(eng, s_ops=s_ops):
                waited = {}
                for o in s_ops:
                    need = {}
                    for d in o["deps"]:
                        p = self.ops[d]
                        key = ("ch", p["ch"]) if p["kind"] == "dma" else (p["stream"], "cmp")
                        v = p["prefix"]
                        if v > waited.get(key, 0):
                            need[key] = max(need.get(key, 0), v)
                    for key, v in need.items():
                        eng.wait_ge(sem_of(key), v)
                        waited[key] = v
                    ins = o["emit"](eng)
                    if o["inc"]:
                        key = (("ch", o["ch"]) if o["kind"] == "dma"
                               else (o["stream"], "cmp"))
                        ins.then_inc(sem_of(key), o["amt"])

            engines_map[s](body)


def _rope_tables():
    i = np.arange(D // 2, dtype=np.float32)
    inv_freq = np.power(np.float32(10000.0),
                        -(i * np.float32(2.0)) / np.float32(D)).astype(np.float32)
    pos = np.arange(N, dtype=np.float32)
    ang = (pos[:, None] * inv_freq[None, :]).astype(np.float32)
    cos = np.cos(ang).astype(np.float32)
    sin = np.sin(ang).astype(np.float32)
    rows = np.arange(128) % 64
    C = cos[:, rows % 32].T.copy()
    Ssig = sin[:, rows % 32].T.copy()
    Ssig[rows >= 32, :] *= -1.0
    return C.astype(np.float32), Ssig.astype(np.float32)


def build_program():
    nc = bass.Bass("TRN2", target_bir_lowering=False, debug=False)

    x_in = nc.dram_tensor("x", [N, E], F32, kind="ExternalInput").ap()
    Wq = nc.dram_tensor("Wq", [E, E], F32, kind="ExternalInput").ap()
    Wk = nc.dram_tensor("Wk", [E, E], F32, kind="ExternalInput").ap()
    Wv = nc.dram_tensor("Wv", [E, E], F32, kind="ExternalInput").ap()
    Wo = nc.dram_tensor("Wo", [E, E], F32, kind="ExternalInput").ap()
    Wg1 = nc.dram_tensor("Wg1", [E, D], F32, kind="ExternalInput").ap()
    Wg2 = nc.dram_tensor("Wg2", [D, E], F32, kind="ExternalInput").ap()
    gamma = nc.dram_tensor("gamma", [E], F32, kind="ExternalInput").ap()
    beta = nc.dram_tensor("beta", [E], F32, kind="ExternalInput").ap()
    r_scale = nc.dram_tensor("r_scale", [E], F32, kind="ExternalInput").ap()
    r_w = nc.dram_tensor("r_w", [E, NEXP], F32, kind="ExternalInput").ap()
    g_w = nc.dram_tensor("g_w", [NEXP, 2, HID, E], F32, kind="ExternalInput").ap()
    l_w = nc.dram_tensor("l_w", [NEXP, HID, E], F32, kind="ExternalInput").ap()
    pe_scale = nc.dram_tensor("pe_scale", [NEXP], F32, kind="ExternalInput").ap()
    y_out = nc.dram_tensor("y", [N, E], F32, kind="ExternalOutput").ap()

    qkv_t = nc.dram_tensor("qkv_t", [24, 128, 8 * 128], BF16, kind="Internal").ap()
    wbf_tmp = nc.dram_tensor("wbf_tmp", [E, E], BF16, kind="Internal").ap()
    gwbf = nc.dram_tensor("gwbf", [NEXP, 2, HID, E], BF16, kind="Internal").ap()
    gwT = nc.dram_tensor("gwT", [NEXP, 2, HID // 128, 128, 8 * 128], BF16,
                         kind="Internal").ap()
    lwbf = nc.dram_tensor("lwbf", [NEXP, HID, E], BF16, kind="Internal").ap()
    wobf = nc.dram_tensor("wobf", [E, E], BF16, kind="Internal").ap()

    C_np, S_np = _rope_tables()
    c_cos = nc.inline_tensor(C_np, name="c_cos").ap()
    c_sin = nc.inline_tensor(S_np, name="c_sin").ap()
    mask_np = (np.arange(128)[:, None] <= np.arange(128)[None, :]).astype(np.float32)
    c_mask = nc.inline_tensor(np.tile(mask_np, (1, 4)).copy(), name="c_mask").ap()
    c_idb = nc.inline_tensor(np.eye(128, dtype=ml_dtypes.bfloat16), name="c_idb").ap()
    c_idf = nc.inline_tensor(np.eye(128, dtype=np.float32), name="c_idf").ap()
    c_ones = nc.inline_tensor(np.ones((1, 128), np.float32), name="c_ones").ap()
    sel_np = np.zeros((8, 8 * 128), np.float32)
    for e in range(8):
        sel_np[e, e * 128:(e + 1) * 128] = 1.0
    c_sel = nc.inline_tensor(sel_np, name="c_sel").ap()
    perm_np = np.zeros((128, 128), np.float32)
    for _i in range(128):
        perm_np[_i ^ 32, _i] = 1.0
    c_perm = nc.inline_tensor(perm_np, name="c_perm").ap()

    A = nc.alloc_sbuf_tensor

    xn_f = A("xn_f", [128, 8 * TS], BF16)
    acc = A("acc", [128, NCH * E], F32)
    Wg1_sb = A("Wg1_sb", [128, 8 * D], BF16)
    Wg2_sb = A("Wg2_sb", [D, E], BF16)
    rwp_sb = A("rwp_sb", [128, 8 * NEXP], F32)
    wqkv_sl = A("wqkv_sl", [128, 4 * 1024], BF16)
    gs0_sl = A("gs0_sl", [128, 4 * 1024], BF16)
    gs1_sl = A("gs1_sl", [128, 4 * 1024], BF16)
    lw_sl = A("lw_sl", [128, 4 * 1024], BF16)
    act_sb = A("act_sb", [128, 16 * TS], BF16)
    Wo_sb = nc.alloc_sbuf_tensor_at(
        "Wo_sb", [128, 8 * E], BF16, offset=nc.lookup_mloc(act_sb).addr)
    q_ro = A("q_ro", [128, 8 * PG], BF16)
    k_ro = A("k_ro", [128, 8 * PG], BF16)
    v_f = A("v_f", [128, 8 * PG], BF16)
    k_tok = A("k_tok", [128, E], BF16)
    v_tok = A("v_tok", [128, E], BF16)
    qs_f = A("qs_f", [128, PG], F32)
    ks_f = A("ks_f", [128, PG], F32)
    rt_f = A("rt_f", [128, 2 * PG], F32)
    rt2_f = A("rt2_f", [128, 2 * PG], F32)
    ctab = A("ctab", [128, PG], F32)
    stab = A("stab", [128, PG], F32)
    Fm = A("Fm", [128, 2 * 512], BF16)
    S_f = A("S_f", [128, 512], F32)
    S_b = A("S_b", [128, 512], BF16)
    g1T = A("g1T", [D, TS], BF16)
    gate_t = A("gate_t", [128, E], F32)
    og = A("og", [128, E], F32)
    z_bf = A("z_bf", [128, E], BF16)
    zT = A("zT", [128, E], BF16)
    sq_scr = A("sq_scr", [128, E], F32)
    gel = A("gel", [128, TS], F32)
    a1 = A("a1", [128, TS], F32)
    c_rep = A("c_rep", [128, TS], F32)
    gam_rep = A("gam_rep", [128, E], F32)
    bWo_rep = A("bWo_rep", [128, E], F32)
    pes_rep = A("pes_rep", [128, NEXP], F32)
    mask_sb = A("mask_sb", [128, 512], F32)
    perm_sb = A("perm_sb", [128, 128], F32)
    idb_sb = A("idb_sb", [128, 128], BF16)
    idf_sb = A("idf_sb", [128, 128], F32)
    ones_sb = A("ones_sb", [1, 128], F32)
    sel_sb = A("sel_sb", [8, 8 * 128], F32)
    xin = A("xin", [128, 2 * E], F32)
    out_sb = A("out_sb", [128, 2 * E], F32)
    cT_sb = A("cT_sb", [8, NCH * 128], F32)
    lg = A("lg", [128, NEXP], F32)
    t8a = A("t8a", [128, NEXP], F32)
    t8b = A("t8b", [128, NEXP], F32)
    c_sb = A("c_sb", [128, NEXP], F32)
    col = {k: A(f"col_{k}", [128, 1], F32) for k in
           ("sum", "sum2", "m", "rstd", "rstd1", "v", "t1", "t2", "m2", "r")}
    coli = A("coli", [128, 1], I32)
    rs8 = A("rs8", [8, 128], F32)
    rsT = A("rsT", [128, 8], F32)
    row_f = A("row_f", [1, E], F32)
    ph0 = A("ph0", [128, 16384], BF16)

    PS = nc.alloc_psum_tensor("PS", [128, 4096], F32)

    def bank(b, lo=0, hi=512):
        return PS[:, b * 512 + lo: b * 512 + hi]

    def bank_bf(b, lo=0, n=64):
        return PS[:, b * 512 + lo: b * 512 + lo + n].bitcast(BF16)

    tk = TK(nc)
    SP, GP, PE, ACT, DVE = "sp", "gp", "pe", "act", "dve"

    taps = {}

    def tap(name, ap, reads):
        if not DEBUG_TAPS:
            return
        t = nc.dram_tensor("tap_" + name, list(ap.shape), ap.dtype,
                           kind="ExternalOutput").ap()
        taps[name] = t
        tk.dma(GP, t, ap, reads=reads)

    tk.dma(GP, mask_sb[:, :], c_mask[:, :], writes=[("mask",)], ch="init")
    tk.dma(GP, perm_sb[:, :], c_perm[:, :], writes=[("perm",)], ch="init")
    tk.dma(GP, idb_sb[:, :], c_idb[:, :], writes=[("idb",)], ch="init")
    tk.dma(GP, idf_sb[:, :], c_idf[:, :], writes=[("idf",)], ch="init")
    tk.dma(GP, ones_sb[:, :], c_ones[:, :], writes=[("ones",)], ch="init")
    tk.dma(GP, sel_sb[:, :], c_sel[:, :], writes=[("sel",)], ch="init")

    PH0R = lambda lo, hi: [("ph0", r) for r in range(lo, hi)]
    for m, W in enumerate((Wq, Wk, Wv)):
        tk.dma(GP, wbf_tmp[:, :], W[:, :], writes=[("wbft",)])
        wall = ph0[:, 0:8192].rearrange("p (dc o) -> p dc o", dc=8)
        tk.dma(GP, wall,
               wbf_tmp.rearrange("(dc p) o -> p dc o", p=128),
               reads=[("wbft",)], writes=PH0R(0, 4))
        for oc in range(8):
            tk.dma(SP, qkv_t[m * 8 + oc, :, :].rearrange("p (dc o) -> p dc o", dc=8),
                   wall[:, :, oc * 128:(oc + 1) * 128],
                   reads=PH0R(0, 4), writes=[("qkvt", m, oc)], ch="qkvt")
    tk.dma(GP, wobf[:, :], Wo[:, :], writes=[("wobf",)], ch="init")

    def emit_wo_load():
        for dc in range(8):
            tk.dma(SP, Wo_sb[:, dc * E:(dc + 1) * E], wobf[dc * 128:(dc + 1) * 128, :],
                   reads=[("wobf",)],
                   writes=[("wo",)] + [("actk", hh) for hh in range(HID // 128)], ch="wo")
    emit_wo_load()
    for dc in range(8):
        tk.dma(GP, Wg1_sb[:, dc * D:(dc + 1) * D], Wg1[dc * 128:(dc + 1) * 128, :],
               writes=[("wg1",)], ch="init")
    tk.dma(GP, Wg2_sb[:, :], Wg2[:, :], writes=[("wg2",)], ch="init")

    def emit_conv_casts(e):
        for i in range(2):
            tk.dma(GP, gwbf[e, i, :, :], g_w[e, i, :, :], writes=[("gwbf", e, i)], ch="gwbf")
        tk.dma(GP, lwbf[e, :, :], l_w[e, :, :], writes=[("lwbf", e)], ch="lwbf")

    def emit_conv_expert(e):
        for i in range(2):
            for dc in range(8):
                tk.dma(SP, ph0[:, dc * 2048:(dc + 1) * 2048],
                       gwbf[e, i, :, dc * 128:(dc + 1) * 128],
                       transpose=True, reads=[("gwbf", e, i)],
                       writes=[("ph0", dc)])
            ph0v = ph0[:, :].rearrange("p (dc h) -> p dc h", dc=8)
            for hc in range(HID // 128):
                tk.dma(SP, gwT[e, i, hc, :, :].rearrange("p (dc h) -> p dc h", dc=8),
                       ph0v[:, :, hc * 128:(hc + 1) * 128],
                       reads=PH0R(0, 8), writes=[("gwT", e, i, hc)],
                       ch=("gwTw", hc % 4))

    tk.dma(GP, rs8[:, :], r_scale.rearrange("(dc p) -> dc p", p=128), writes=[("rs8",)], ch="init")
    tk.op(PE, "transpose", bank(1, 0, 8), rs8[:, :], idf_sb[0:8, 0:8],
          reads=[("rs8",), ("idf",)], writes=[("ps", 1)])
    tk.op(DVE, "tensor_copy", rsT[:, :], bank(1, 0, 8),
          reads=[("ps", 1)], writes=[("rsT",)])
    for dc in range(8):
        tk.dma(GP, lg[:, :], r_w[dc * 128:(dc + 1) * 128, :], writes=[("lg",)], ch="init")
        tk.op(DVE, "scalar_tensor_tensor", rwp_sb[:, dc * NEXP:(dc + 1) * NEXP],
              lg[:, :], RSQRT_D,
              rsT[:, dc:dc + 1].broadcast_to((128, NEXP)), ALU.mult, ALU.mult,
              reads=[("lg",), ("rsT",)], writes=[("rwp",)])

    tk.dma(GP, row_f[:, :], gamma.unsqueeze(0), writes=[("rowf",)], ch="init")
    for j in range(2):
        tk.op(PE, "matmul", bank(2 + j), ones_sb[:, :], row_f[:, j * 512:(j + 1) * 512],
              start=True, stop=True, reads=[("rowf",), ("ones",)], writes=[("ps", 2 + j)])
    tk.op(DVE, "tensor_copy", gam_rep[:, :], PS[:, 2 * 512:4 * 512],
          reads=[("ps", 2), ("ps", 3)], writes=[("gam",)])
    tk.dma(GP, row_f[0:1, 0:NEXP], pe_scale.unsqueeze(0), writes=[("rowf",)], ch="init")
    tk.op(PE, "matmul", bank(2, 0, NEXP), ones_sb[:, :], row_f[0:1, 0:NEXP],
          start=True, stop=True, reads=[("rowf",), ("ones",)], writes=[("ps", 2)])
    tk.op(DVE, "tensor_copy", pes_rep[:, :], bank(2, 0, NEXP),
          reads=[("ps", 2)], writes=[("pes",)])
    tk.dma(GP, rs8[:, :], beta.rearrange("(dc p) -> dc p", p=128), writes=[("rs8",)], ch="init")
    tk.op(PE, "transpose", bank(1, 0, 8), rs8[:, :], idf_sb[0:8, 0:8],
          reads=[("rs8",), ("idf",)], writes=[("ps", 1)])
    tk.op(DVE, "tensor_copy", rsT[:, :], bank(1, 0, 8),
          reads=[("ps", 1)], writes=[("rsT",)])
    tk.op(DVE, "tensor_copy", z_bf[:, 0:8], rsT[:, :], reads=[("rsT",)],
          writes=[("znb",)])
    for j in range(2):
        for dc in range(8):
            tk.op(PE, "matmul", PS[0:1, (4 + j) * 512:(5 + j) * 512],
                  z_bf[:, dc:dc + 1],
                  Wo_sb[:, dc * E + j * 512: dc * E + (j + 1) * 512],
                  start=(dc == 0), stop=(dc == 7),
                  reads=[("znb",), ("wo",)], writes=[("ps", 4 + j)])
    tk.op(DVE, "tensor_copy", row_f[:, :], PS[0:1, 4 * 512:6 * 512],
          reads=[("ps", 4), ("ps", 5)], writes=[("rowf",)])
    for j in range(2):
        tk.op(PE, "matmul", bank(2 + j), ones_sb[:, :], row_f[:, j * 512:(j + 1) * 512],
              start=True, stop=True, reads=[("rowf",), ("ones",)], writes=[("ps", 2 + j)])
    tk.op(DVE, "tensor_copy", bWo_rep[:, :], PS[:, 2 * 512:4 * 512],
          reads=[("ps", 2), ("ps", 3)], writes=[("bwo",)])

    def newton_rsqrt(vcol, outcol, tags_r, tag_w):
        tk.op(DVE, "tensor_scalar", coli[:, :], vcol.bitcast(I32), 1, None,
              ALU.arith_shift_right, reads=tags_r, writes=[("coli",)])
        tk.op(DVE, "tensor_scalar", coli[:, :], coli[:, :], -1, MAGIC,
              ALU.mult, ALU.add, reads=[("coli",)], writes=[("coli",)])
        for it in range(NEWTON_ITERS):
            tk.op(DVE, "tensor_mul", col["t1"][:, :], coli[:, :].bitcast(F32),
                  coli[:, :].bitcast(F32), reads=[("coli",)], writes=[("c_t1",)])
            tk.op(DVE, "tensor_mul", col["t1"][:, :], col["t1"][:, :], vcol,
                  reads=[("c_t1",)] + list(tags_r), writes=[("c_t1",)])
            tk.op(DVE, "tensor_scalar", col["t1"][:, :], col["t1"][:, :], -0.5, 1.5,
                  ALU.mult, ALU.add, reads=[("c_t1",)], writes=[("c_t1",)])
            last = (it == NEWTON_ITERS - 1)
            tk.op(DVE, "tensor_tensor",
                  outcol if last else coli[:, :].bitcast(F32),
                  coli[:, :].bitcast(F32), col["t1"][:, :], ALU.mult,
                  reads=[("coli",), ("c_t1",)],
                  writes=[tag_w if last else ("coli",)])

    tap("rwp", rwp_sb[:, :], [("rwp",)])
    tap("rowf", row_f[:, :], [("rowf",)])
    tap("rsT2", rsT[:, :], [("rsT",)])
    tap("zb8", z_bf[:, 0:8], [("znb",)])
    tap("gam", gam_rep[:, :], [("gam",)])
    tap("bwo", bWo_rep[:, :], [("bwo",)])
    tap("pes", pes_rep[:, :], [("pes",)])
    tap("wo0", Wo_sb[:, 0:E], [("wo",)])
    tap("qkvt0", qkv_t[0, :, :], [("qkvt", 0, 0)])
    tap("qkvt8", qkv_t[8, :, :], [("qkvt", 1, 0)])
    tap("gwt0", gwT[0, 0, 0, :, :], [("gwT", 0, 0, 0)])
    tap("lwbf0", lwbf[0, 0:128, :], [("lwbf", 0)])

    tk.op(DVE, "memset", S_f[:, :], 0.0, writes=[("sf",)])
    tk.op(DVE, "memset", S_b[:, :], 0.0, writes=[("sb",)])

    def emit_passA(st):
        t0 = st * TS

        for i in range(NCH):
            par = i % 2
            xv = xin[:, par * E:(par + 1) * E]
            tk.dma(ACT, xv, x_in[t0 + i * CH: t0 + (i + 1) * CH, :],
                   writes=[("xin", par)])
            tk.op(ACT, "activation", og[:, :], xv, AF.Copy,
                  accum_out=col["sum"][:, :],
                  reads=[("xin", par)], writes=[("og",), ("csum",)])
            tk.op(ACT, "activation", sq_scr[:, :], xv, AF.Square,
                  accum_out=col["sum2"][:, :],
                  reads=[("xin", par)], writes=[("sqscr",), ("csum2",)])
            tk.op(DVE, "tensor_scalar_mul", col["m"][:, :], col["sum"][:, :],
                  1.0 / E, reads=[("csum",)], writes=[("cm",)])
            tk.op(DVE, "tensor_mul", col["t2"][:, :], col["m"][:, :], col["m"][:, :],
                  reads=[("cm",)], writes=[("ct2",)])
            tk.op(DVE, "scalar_tensor_tensor", col["v"][:, :], col["sum2"][:, :],
                  1.0 / E, col["t2"][:, :], ALU.mult, ALU.subtract,
                  reads=[("csum2",), ("ct2",)], writes=[("cv",)])
            tk.op(DVE, "tensor_scalar_add", col["v"][:, :], col["v"][:, :], EPS1,
                  reads=[("cv",)], writes=[("cv",)])
            newton_rsqrt(col["v"][:, :], col["rstd"][:, :], [("cv",)], ("crstd", 0))
            tk.op(DVE, "tensor_scalar", sq_scr[:, :], xv, col["m"][:, :],
                  col["rstd"][:, :], ALU.subtract, ALU.mult,
                  reads=[("xin", par), ("cm",), ("crstd", 0)], writes=[("sqscr",)])
            tk.op(ACT, "activation", z_bf[:, :], sq_scr[:, :], AF.Copy,
                  reads=[("sqscr",)], writes=[("znb",)])
            for dc in range(8):
                tk.op(PE, "transpose", bank_bf(7, dc * 64, 64),
                      z_bf[:, dc * 128:(dc + 1) * 128], idb_sb[:, :],
                      reads=[("znb",), ("idb",)], writes=[("ps", 7)])
            tk.op(DVE, "tensor_copy",
                  xn_f[:, :].rearrange("p (dc t) -> p dc t", dc=8)[
                      :, :, i * CH:(i + 1) * CH],
                  bank_bf(7, 0, 512).rearrange("p (a b) -> p a b", a=8),
                  reads=[("ps", 7)],
                  writes=[("xnf", dc, i) for dc in range(8)])
            tk.op(DVE, "reciprocal", col["t1"][:, :], col["v"][:, :],
                  reads=[("cv",)], writes=[("c_t1",)])
            tk.op(DVE, "tensor_scalar", col["t2"][:, :], col["t1"][:, :], -EPS1,
                  1.0, ALU.mult, ALU.add, reads=[("c_t1",)], writes=[("ct2",)])
            tk.op(DVE, "tensor_scalar_add", col["t2"][:, :], col["t2"][:, :], EPS2,
                  reads=[("ct2",)], writes=[("ct2",)])
            newton_rsqrt(col["t2"][:, :], col["r"][:, :], [("ct2",)], ("c_r",))
            for half in range(2):
                for s4 in range(4):
                    dc = half * 4 + s4
                    tk.op(PE, "transpose",
                          PS[:, 7 * 512 + s4 * 128: 7 * 512 + (s4 + 1) * 128],
                          sq_scr[:, dc * 128:(dc + 1) * 128], idf_sb[:, :],
                          reads=[("sqscr",), ("idf",)], writes=[("ps", 7)])
                tk.op(DVE, "tensor_copy",
                      og[:, half * 512:(half + 1) * 512],
                      PS[:, 7 * 512:7 * 512 + 512],
                      reads=[("ps", 7)], writes=[("og",)])
            for dc in range(8):
                tk.op(PE, "matmul", bank(0, 0, NEXP),
                      og[:, dc * 128:(dc + 1) * 128],
                      rwp_sb[:, dc * NEXP:(dc + 1) * NEXP],
                      start=(dc == 0), stop=(dc == 7),
                      reads=[("og",), ("rwp",)], writes=[("ps", 0)])
            tk.op(DVE, "tensor_scalar_mul", lg[:, :], bank(0, 0, NEXP),
                  col["r"][:, :], reads=[("ps", 0), ("c_r",)], writes=[("lg",)])
            tk.op(DVE, "reduce_max", col["m"][:, :], lg[:, :], axis=AX.X,
                  reads=[("lg",)], writes=[("cm",)])
            tk.op(DVE, "tensor_scalar", t8a[:, :], lg[:, :], col["m"][:, :], None,
                  ALU.is_ge, reads=[("lg",), ("cm",)], writes=[("t8a",)])
            tk.op(DVE, "scalar_tensor_tensor", t8b[:, :], t8a[:, :], -1e30,
                  lg[:, :], ALU.mult, ALU.add, reads=[("t8a",), ("lg",)],
                  writes=[("t8b",)])
            tk.op(DVE, "reduce_max", col["m2"][:, :], t8b[:, :], axis=AX.X,
                  reads=[("t8b",)], writes=[("cm2",)])
            tk.op(DVE, "tensor_scalar_mul", col["t1"][:, :], col["m"][:, :], -1.0,
                  reads=[("cm",)], writes=[("c_t1",)])
            tk.op(ACT, "activation", col["t2"][:, :], col["m2"][:, :], AF.Exp,
                  bias=col["t1"][:, :], scale=1.0,
                  reads=[("cm2",), ("c_t1",)], writes=[("ct2",)])
            tk.op(DVE, "tensor_scalar_add", col["t2"][:, :], col["t2"][:, :], 1.0,
                  reads=[("ct2",)], writes=[("ct2",)])
            tk.op(DVE, "reciprocal", col["r"][:, :], col["t2"][:, :],
                  reads=[("ct2",)], writes=[("c_r",)])
            tk.op(ACT, "activation", t8a[:, :], lg[:, :], AF.Exp,
                  bias=col["t1"][:, :], scale=1.0,
                  reads=[("lg",), ("c_t1",)], writes=[("t8a",)])
            tk.op(DVE, "tensor_scalar", t8b[:, :], lg[:, :], col["m2"][:, :], None,
                  ALU.is_ge, reads=[("lg",), ("cm2",)], writes=[("t8b",)])
            tk.op(DVE, "tensor_mul", t8a[:, :], t8a[:, :], t8b[:, :],
                  reads=[("t8a",), ("t8b",)], writes=[("t8a",)])
            tk.op(DVE, "tensor_scalar_mul", t8a[:, :], t8a[:, :], col["r"][:, :],
                  reads=[("t8a",), ("c_r",)], writes=[("t8a",)])
            tk.op(DVE, "tensor_mul", c_sb[:, :], t8a[:, :], pes_rep[:, :],
                  reads=[("t8a",), ("pes",)], writes=[("csb",)])
            tk.op(PE, "transpose", PS[0:NEXP, 1 * 512:1 * 512 + 128], c_sb[:, :],
                  idf_sb[:, :], reads=[("csb",), ("idf",)], writes=[("ps", 1)])
            tk.op(DVE, "tensor_copy", cT_sb[:, i * 128:(i + 1) * 128],
                  PS[0:NEXP, 1 * 512:1 * 512 + 128],
                  reads=[("ps", 1)], writes=[("cT", i)])

        if st == 0:
            tap("xnf", xn_f[:, :],
                [("xnf", dc, ii) for dc in range(8) for ii in range(NCH)])
            tap("ct", cT_sb[:, :], [("cT", ii) for ii in range(NCH)])

    def emit_attn(st):
        t0 = st * TS
        pend = [None]
        if st >= 1:
            emit_wo_load()
        for dc in range(8):
            tk.op(PE, "matmul", PS[0:D, 0:TS], Wg1_sb[:, dc * D:(dc + 1) * D],
                  xn_f[:, dc * TS:(dc + 1) * TS], start=(dc == 0), stop=(dc == 7),
                  reads=[("wg1",)] + [("xnf", dc, ii) for ii in range(NCH)],
                  writes=[("ps", 0)])
        tk.op(DVE, "tensor_copy", g1T[:, :], PS[0:D, 0:TS],
              reads=[("ps", 0)], writes=[("g1T",)])

        for g in range(NPG):
            gt0 = t0 + g * PG
            tk.dma(ACT, ctab[:, :], c_cos[:, gt0:gt0 + PG], writes=[("ctab",)])
            tk.dma(ACT, stab[:, :], c_sin[:, gt0:gt0 + PG], writes=[("stab",)])
            for oc in range(24):
                par = oc % 4
                sl = wqkv_sl[:, par * 1024:(par + 1) * 1024]
                tk.dma(SP, sl, qkv_t[oc, :, :], writes=[("wsl", par)])
                for dc in range(8):
                    tk.op(PE, "matmul", bank(par, 0, PG),
                          sl[:, dc * 128:(dc + 1) * 128],
                          xn_f[:, dc * TS + g * PG: dc * TS + (g + 1) * PG],
                          start=(dc == 0), stop=(dc == 7),
                          reads=[("wsl", par)] +
                                [("xnf", dc, g * 2), ("xnf", dc, g * 2 + 1)],
                          writes=[("ps", par)])
                m, fc = divmod(oc, 8)
                if m < 2:
                    s_out = qs_f if m == 0 else ks_f
                    s_tag = ("qs",) if m == 0 else ("ks",)
                    dst = q_ro if m == 0 else k_ro
                    d_tag = ("qro", fc) if m == 0 else ("kro", fc)
                    swb = 4 + oc % 2
                    p2 = oc % 2
                    rtv = rt_f[:, p2 * PG:(p2 + 1) * PG]
                    rt2v = rt2_f[:, p2 * PG:(p2 + 1) * PG]
                    tk.op(ACT, "activation", s_out[:, :], bank(par, 0, PG),
                          AF.Silu, reads=[("ps", par)], writes=[s_tag])
                    tk.op(PE, "matmul", bank(swb, 0, PG), perm_sb[:, :],
                          s_out[:, :], start=True, stop=True,
                          reads=[s_tag, ("perm",)], writes=[("ps", swb)])
                    tk.op(DVE, "tensor_mul", rtv, s_out[:, :], ctab[:, :],
                          reads=[s_tag, ("ctab",)], writes=[("rt", p2)])
                    tk.op(DVE, "tensor_mul", rt2v, bank(swb, 0, PG),
                          stab[:, :], reads=[("ps", swb), ("stab",)],
                          writes=[("rt2", p2)])
                    tk.op(DVE, "tensor_sub", dst[:, fc * PG:(fc + 1) * PG],
                          rtv, rt2v,
                          reads=[("rt", p2), ("rt2", p2)], writes=[d_tag])
                else:
                    tk.op(ACT, "activation", v_f[:, fc * PG:(fc + 1) * PG],
                          bank(par, 0, PG), AF.Copy,
                          reads=[("ps", par)], writes=[("vf", fc)])

            if st == 0 and g == 0:
                tap("qro", q_ro[:, :], [("qro", fc) for fc in range(8)])
                tap("kro", k_ro[:, :], [("kro", fc) for fc in range(8)])
                tap("vfm", v_f[:, :], [("vf", fc) for fc in range(8)])
                tap("g1t", g1T[:, :], [("g1T",)])

            for cc in range(2):
                i = g * 2 + cc
                for fc in range(8):
                    tk.op(PE, "transpose", bank_bf(7, fc * 64, 64),
                          k_ro[:, fc * PG + cc * 128: fc * PG + (cc + 1) * 128],
                          idb_sb[:, :],
                          reads=[("kro", fc), ("idb",)], writes=[("ps", 7)])
                tk.op(DVE, "tensor_copy", k_tok[:, :], bank_bf(7, 0, 512),
                      reads=[("ps", 7)],
                      writes=[("ktok", fc) for fc in range(8)])
                for fc in range(8):
                    tk.op(PE, "transpose", bank_bf(7, fc * 64, 64),
                          v_f[:, fc * PG + cc * 128: fc * PG + (cc + 1) * 128],
                          idb_sb[:, :],
                          reads=[("vf", fc), ("idb",)], writes=[("ps", 7)])
                tk.op(DVE, "tensor_copy", v_tok[:, :], bank_bf(7, 0, 512),
                      reads=[("ps", 7)],
                      writes=[("vtok", fc) for fc in range(8)])
                for j in range(2):
                    tk.op(PE, "matmul", bank(j, 0, 512),
                          g1T[:, i * 128:(i + 1) * 128],
                          Wg2_sb[:, j * 512:(j + 1) * 512],
                          start=True, stop=True,
                          reads=[("g1T",), ("wg2",)], writes=[("ps", j)])
                tk.op(ACT, "activation", gate_t[:, :], PS[:, 0:1024], AF.Sigmoid,
                      reads=[("ps", 0), ("ps", 1)], writes=[("gatet",)])
                for hg in range(4):
                    for hh in range(4):
                        h = hg * 4 + hh
                        fc, half = divmod(h, 2)
                        rows = slice(half * 64, half * 64 + 64)
                        fb = 2 + h % 2
                        fcol = (hg % 2) * 512 + hh * 128
                        tk.op(PE, "matmul", bank(fb, 0, 128),
                              k_ro[rows, fc * PG + cc * 128: fc * PG + (cc + 1) * 128],
                              q_ro[rows, fc * PG + cc * 128: fc * PG + (cc + 1) * 128],
                              start=True, stop=True,
                              reads=[("kro", fc), ("qro", fc)], writes=[("ps", fb)])
                        tk.op(DVE, "tensor_mul", Fm[:, fcol:fcol + 128],
                              bank(fb, 0, 128), mask_sb[:, 0:128],
                              reads=[("ps", fb), ("mask",)],
                              writes=[("fm", hg % 2)])
                    for hh in range(4):
                        h = hg * 4 + hh
                        fc, half = divmod(h, 2)
                        rows = slice(half * 64, half * 64 + 64)
                        ob, olo = divmod(h * 64, 512)
                        fcol = (hg % 2) * 512 + hh * 128
                        tk.op(PE, "matmul", bank(4 + ob, olo, olo + 64),
                              Fm[:, fcol:fcol + 128], v_tok[:, h * 64:(h + 1) * 64],
                              start=True, stop=False,
                              reads=[("fm", hg % 2), ("vtok", h // 2)],
                              writes=[("ps", 4 + ob)])
                        tk.op(PE, "matmul", bank(4 + ob, olo, olo + 64),
                              q_ro[rows, fc * PG + cc * 128: fc * PG + (cc + 1) * 128],
                              S_b[rows, (h // 2) * 64:(h // 2) * 64 + 64],
                              start=False, stop=True,
                              reads=[("qro", fc), ("sb",)], writes=[("ps", 4 + ob)])
                for h in range(H):
                    half = h % 2
                    tk.op(PE, "matmul",
                          PS[half * 64:half * 64 + 64,
                             6 * 512 + (h // 2) * 64: 6 * 512 + (h // 2) * 64 + 64],
                          k_tok[:, h * 64:(h + 1) * 64], v_tok[:, h * 64:(h + 1) * 64],
                          start=(h < 2), stop=(h >= H - 2),
                          reads=[("ktok", h // 2), ("vtok", h // 2)],
                          writes=[("ps", 6)])
                tk.op(DVE, "tensor_add", S_f[:, :], S_f[:, :], bank(6),
                      reads=[("ps", 6), ("sf",)], writes=[("sf",)])
                tk.op(DVE, "tensor_copy", S_b[:, :], S_f[:, :],
                      reads=[("sf",)], writes=[("sb",)])
                tk.op(DVE, "tensor_mul", og[:, :], PS[:, 4 * 512:6 * 512],
                      gate_t[:, :],
                      reads=[("ps", 4), ("ps", 5), ("gatet",)], writes=[("og",)])
                tk.op(ACT, "activation", gate_t[:, :], og[:, :], AF.Copy,
                      accum_out=col["sum"][:, :],
                      reads=[("og",), ("gatet",)], writes=[("gatet",), ("csum",)])
                tk.op(ACT, "activation", sq_scr[:, :], og[:, :], AF.Square,
                      accum_out=col["sum2"][:, :],
                      reads=[("og",)], writes=[("sqscr",), ("csum2",)])
                rcol = col["rstd"] if i % 2 == 0 else col["rstd1"]
                rtag = ("crstd", i % 2)

                def s1b(i=i, rcol=rcol, rtag=rtag):
                    tk.op(DVE, "tensor_scalar_mul", col["m"][:, :], col["sum"][:, :],
                          1.0 / E, reads=[("csum",)], writes=[("cm",)])
                    tk.op(DVE, "tensor_mul", col["t2"][:, :], col["m"][:, :],
                          col["m"][:, :], reads=[("cm",)], writes=[("ct2",)])
                    tk.op(DVE, "scalar_tensor_tensor", col["v"][:, :], col["sum2"][:, :],
                          1.0 / E, col["t2"][:, :], ALU.mult, ALU.subtract,
                          reads=[("csum2",), ("ct2",)], writes=[("cv",)])
                    tk.op(DVE, "tensor_scalar_add", col["v"][:, :], col["v"][:, :], EPSL,
                          reads=[("cv",)], writes=[("cv",)])
                    newton_rsqrt(col["v"][:, :], rcol[:, :], [("cv",)], rtag)
                    tk.op(DVE, "scalar_tensor_tensor", z_bf[:, :], og[:, :],
                          col["m"][:, :], gam_rep[:, :], ALU.subtract, ALU.mult,
                          reads=[("og",), ("cm",), ("gam",)], writes=[("znb",)])

                if pend[0] is not None:
                    pend[0]()
                s1b()
                if st == 0 and i == 0:
                    tap("ktok0", k_tok[:, :], [("ktok", fc) for fc in range(8)])
                    tap("vtok0", v_tok[:, :], [("vtok", fc) for fc in range(8)])
                    tap("fm0", Fm[:, :], [("fm", h) for h in range(H)])
                    tap("sf0", S_f[:, :], [("sf",)])
                    tap("gatet0", gate_t[:, :], [("gatet",)])
                    tap("og0", og[:, :], [("og",)])
                    tap("zbf0", z_bf[:, :], [("znb",)])

                def epiB(i=i, rcol=rcol, rtag=rtag):
                    for dc in range(8):
                        tk.op(PE, "transpose", bank_bf(7, dc * 64, 64),
                              z_bf[:, dc * 128:(dc + 1) * 128], idb_sb[:, :],
                              reads=[("znb",), ("idb",)], writes=[("ps", 7)])
                    tk.op(DVE, "tensor_copy", zT[:, :], bank_bf(7, 0, 512),
                          reads=[("ps", 7)],
                          writes=[("zT", dc) for dc in range(8)])
                    for jh in range(2):
                        for dc in range(8):
                            tk.op(PE, "matmul", bank(jh, 0, 512),
                                  zT[:, dc * 128:(dc + 1) * 128],
                                  Wo_sb[:, dc * E + jh * 512: dc * E + (jh + 1) * 512],
                                  start=(dc == 0), stop=(dc == 7),
                                  reads=[("zT", dc), ("wo",)], writes=[("ps", jh)])
                    tk.op(DVE, "scalar_tensor_tensor", acc[:, i * E:(i + 1) * E],
                          PS[:, 0:1024], rcol[:, :], bWo_rep[:, :],
                          ALU.mult, ALU.add,
                          reads=[("ps", 0), ("ps", 1), rtag, ("bwo",)],
                          writes=[("acc", i)])
                    if st == 0 and i == 0:
                        tap("acc0", acc[:, 0:E], [("acc", 0)])

                pend[0] = epiB
        pend[0]()
        pend[0] = None
        tap(f"sfst{st}", S_f[:, :], [("sf",)])
        tap(f"xnfst{st}", xn_f[:, 0:512], [("xnf", 0, ii) for ii in range(NCH)])

    def emit_ffn_expert(st, e):
        if True:
            tk.op(PE, "matmul", bank(6, 0, 512), sel_sb[:, e * 128:(e + 1) * 128],
                  cT_sb[:, :], start=True, stop=True,
                  reads=[("sel",)] + [("cT", ii) for ii in range(NCH)],
                  writes=[("ps", 6)])
            tk.op(DVE, "tensor_copy", c_rep[:, :], bank(6, 0, 512),
                  reads=[("ps", 6)], writes=[("crep",)])
            for hc in range(HID // 128):
                par = hc % 4
                g0b, g1b = (hc % 2) * 2, (hc % 2) * 2 + 1
                s0 = gs0_sl[:, par * 1024:(par + 1) * 1024]
                s1 = gs1_sl[:, par * 1024:(par + 1) * 1024]
                tk.dma(SP, s0, gwT[e, 0, hc, :, :], writes=[("gs0", par)])
                tk.dma(SP, s1, gwT[e, 1, hc, :, :], writes=[("gs1", par)])
                for dc in range(8):
                    tk.op(PE, "matmul", bank(g0b), s0[:, dc * 128:(dc + 1) * 128],
                          xn_f[:, dc * TS:(dc + 1) * TS],
                          start=(dc == 0), stop=(dc == 7),
                          reads=[("gs0", par)] + [("xnf", dc, ii) for ii in range(NCH)],
                          writes=[("ps", g0b)])
                for dc in range(8):
                    tk.op(PE, "matmul", bank(g1b), s1[:, dc * 128:(dc + 1) * 128],
                          xn_f[:, dc * TS:(dc + 1) * TS],
                          start=(dc == 0), stop=(dc == 7),
                          reads=[("gs1", par)] + [("xnf", dc, ii) for ii in range(NCH)],
                          writes=[("ps", g1b)])
                tk.op(ACT, "activation", gel[:, :], bank(g0b), AF.Gelu_apprx_tanh,
                      reads=[("ps", g0b)], writes=[("gel",)])
                tk.op(DVE, "tensor_mul", a1[:, :], gel[:, :], bank(g1b),
                      reads=[("gel",), ("ps", g1b)], writes=[("a1",)])
                tk.op(GP, "tensor_tensor", act_sb[:, hc * TS:(hc + 1) * TS],
                      a1[:, :], c_rep[:, :], ALU.mult,
                      reads=[("a1",), ("crep",)], writes=[("actk", hc), ("wo",)])
            for hc in range(HID // 128):
                par = hc % 4
                sl = lw_sl[:, par * 1024:(par + 1) * 1024]
                tk.dma(SP, sl, lwbf[e, hc * 128:(hc + 1) * 128, :],
                       writes=[("lwsl", par)])
                for isl in range(4):
                    for jh in range(2):
                        b = isl * 2 + jh
                        tk.op(PE, "matmul", bank(b),
                              act_sb[:, hc * TS + isl * 128: hc * TS + (isl + 1) * 128],
                              sl[:, jh * 512:(jh + 1) * 512],
                              start=(hc == 0), stop=(hc == HID // 128 - 1),
                              reads=[("actk", hc), ("lwsl", par)],
                              writes=[("ps", b)])
            for isl in range(4):
                tk.op(DVE, "tensor_add", acc[:, isl * E:(isl + 1) * E],
                      acc[:, isl * E:(isl + 1) * E], PS[:, isl * 1024:(isl + 1) * 1024],
                      reads=[("ps", isl * 2), ("ps", isl * 2 + 1), ("acc", isl)],
                      writes=[("acc", isl)])
            if st == 0 and e == 0:
                tap("actsb0", act_sb[:, :],
                    [("actk", hh) for hh in range(HID // 128)])
                tap("crep0", c_rep[:, :], [("crep",)])
                tap("accf0", acc[:, 0:E], [("acc", 0)])
            if st == 0 and e == 7:
                tap("actsb7", act_sb[:, :],
                    [("actk", hh) for hh in range(HID // 128)])
                tap("crep7", c_rep[:, :], [("crep",)])
                tap("accful", acc[:, :], [("acc", ii) for ii in range(NCH)])
            if e == 7:
                tap(f"accst{st}", acc[:, :], [("acc", ii) for ii in range(NCH)])

    def emit_store(st):
        t0 = st * TS
        for i in range(NCH):
            par = i % 2
            ov = out_sb[:, par * E:(par + 1) * E]
            tk.dma(ACT, ov, x_in[t0 + i * CH: t0 + (i + 1) * CH, :],
                   writes=[("outs", par)])
            tk.op(DVE, "tensor_add", ov, acc[:, i * E:(i + 1) * E], ov,
                  reads=[("outs", par), ("acc", i)], writes=[("outs", par)])
            tk.dma(ACT, y_out[t0 + i * CH: t0 + (i + 1) * CH, :], ov,
                   reads=[("outs", par)], writes=[("y", st, i)], ch="y")

    emit_passA(0)
    for e in range(NEXP):
        emit_conv_casts(e)
    emit_attn(0)
    emit_conv_expert(0)
    emit_conv_expert(1)
    for e in range(NEXP):
        if e + 2 < NEXP:
            emit_conv_expert(e + 2)
        emit_ffn_expert(0, e)
    for st in range(1, NST):
        emit_passA(st)
        emit_store(st - 1)
        emit_attn(st)
        for e in range(NEXP):
            emit_ffn_expert(st, e)
    emit_store(NST - 1)

    with nc.Block() as block:
        engines_map = dict(sp=block.sync, gp=block.gpsimd, pe=block.tensor,
                           act=block.scalar, dve=block.vector)
        tk.finalize_and_emit(block, engines_map)
        def final_sync(eng):
            for key, tot in tk.totals.items():
                if tot > 0:
                    sem = tk.ch_sems[key[1]] if key[0] == "ch" else tk.sems[key]
                    eng.wait_ge(sem, tot)
        block.sync(final_sync)

    return nc


_S_INIT_FIXED = False


def kernel(L0_attn_x, L0_attn_Wq, L0_attn_Wk, L0_attn_Wv, L0_attn_Wo,
           L0_attn_Wg1, L0_attn_Wg2, L0_attn_gamma, L0_attn_beta,
           L0_ffn_router_scale, L0_ffn_router_w, L0_ffn_gating_w,
           L0_ffn_linear_w, L0_ffn_per_expert_scale):
    x = np.ascontiguousarray(np.asarray(L0_attn_x, dtype=np.float32))
    common = {
        "Wq": np.ascontiguousarray(np.asarray(L0_attn_Wq, np.float32)),
        "Wk": np.ascontiguousarray(np.asarray(L0_attn_Wk, np.float32)),
        "Wv": np.ascontiguousarray(np.asarray(L0_attn_Wv, np.float32)),
        "Wo": np.ascontiguousarray(np.asarray(L0_attn_Wo, np.float32)),
        "Wg1": np.ascontiguousarray(np.asarray(L0_attn_Wg1, np.float32)),
        "Wg2": np.ascontiguousarray(np.asarray(L0_attn_Wg2, np.float32)),
        "gamma": np.ascontiguousarray(np.asarray(L0_attn_gamma, np.float32)),
        "beta": np.ascontiguousarray(np.asarray(L0_attn_beta, np.float32)),
        "r_scale": np.ascontiguousarray(np.asarray(L0_ffn_router_scale, np.float32)),
        "r_w": np.ascontiguousarray(np.asarray(L0_ffn_router_w, np.float32)),
        "g_w": np.ascontiguousarray(np.asarray(L0_ffn_gating_w, np.float32)),
        "l_w": np.ascontiguousarray(np.asarray(L0_ffn_linear_w, np.float32)),
        "pe_scale": np.ascontiguousarray(np.asarray(L0_ffn_per_expert_scale,
                                                    np.float32)),
    }
    nc = build_program()
    in_maps = [dict(common, x=np.ascontiguousarray(x[b])) for b in range(NCORES)]
    res = run_bass_kernel_spmd(nc, in_maps, core_ids=list(range(NCORES)))
    return np.stack([np.asarray(res.results[b]["y"], np.float32)
                     for b in range(NCORES)], axis=0)
